# version 12
# baseline (speedup 1.0000x reference)
import numpy as np
import ml_dtypes
import concourse.bass as bass
import concourse.mybir as mybir
from concourse.bass_utils import run_bass_kernel_spmd

F32 = mybir.dt.float32
BF16 = mybir.dt.bfloat16
FP16 = mybir.dt.float16
I32 = mybir.dt.int32
U16 = mybir.dt.uint16
I8 = mybir.dt.int8
AF = mybir.ActivationFunctionType
ALU = mybir.AluOpType

D = 2048
I_FFN = 5632
TK = 4096
TB = 512
NTB = TK // TB
NK = D // 128
NKI = I_FFN // 128
LN_EPS = 1e-5
RMS_EPS = 1e-5
FFN_EPS = 1e-6
MAGIC = 0x5F3759DF


class Evt:
    __slots__ = ("sem", "val", "eng", "dma")

    def __init__(self, sem, val, eng, dma):
        self.sem, self.val, self.eng, self.dma = sem, val, eng, dma


class Prog:
    SEMS = ("SD", "GD", "PE", "DV", "AC", "CC")

    def __init__(self):
        self.steps = []
        self.counts = {s: 0 for s in self.SEMS}

    def op(self, eng, fn, waits=(), sem=None):
        waits = tuple(w for w in waits if w is not None)
        self.steps.append((eng, fn, waits, sem))
        if sem is None:
            return None
        inc = 16 if sem in ("SD", "GD") else 1
        self.counts[sem] += inc
        return Evt(sem, self.counts[sem], eng, sem in ("SD", "GD", "CC"))

    def barrier(self, sems=("PE", "DV", "AC", "SD", "GD")):
        return [Evt(s, self.counts[s], "none", True) for s in sems
                if self.counts[s] > 0]

    def emit_engine(self, eng_name, eng_obj, handles):
        waited = {}
        for eng, fn, waits, sem in self.steps:
            if eng != eng_name:
                continue
            for w in waits:
                if w.eng == eng_name and not w.dma:
                    continue
                if waited.get(w.sem, 0) >= w.val:
                    continue
                eng_obj.wait_ge(handles[w.sem], w.val)
                waited[w.sem] = w.val
            instr = fn(eng_obj)
            if sem is not None:
                inc = 16 if sem in ("SD", "GD") else 1
                instr.then_inc(handles[sem], inc)


def _perm_H():
    kk = np.arange(D)
    kappa, p = kk // 128, kk % 128
    c, j = kappa // 8, kappa % 8
    return 8 * (128 * c + p) + j


def _perm_I():
    out = []
    for c in range(6):
        size = 128 if c < 5 else 64
        for j in range(8):
            for p in range(size):
                out.append(8 * (128 * c + p) + j)
    return np.asarray(out)


def _wtile(W):
    return np.ascontiguousarray(W.reshape(16, 128, 16, 128).transpose(2, 1, 0, 3))


def build(debug=False):
    nc = bass.Bass("TRN2", target_bir_lowering=False)

    x_in = nc.dram_tensor("x", [TK, D], F32, kind="ExternalInput").ap()
    wi_in = nc.dram_tensor("wi", [16, 128, 16, 128], F32, kind="ExternalInput").ap()
    wf_in = nc.dram_tensor("wf", [16, 128, 16, 128], F32, kind="ExternalInput").ap()
    wg_in = nc.dram_tensor("wg", [16, 128, 16, 128], F32, kind="ExternalInput").ap()
    wo_in = nc.dram_tensor("wo", [16, 128, 16, 128], F32, kind="ExternalInput").ap()
    gnw_in = nc.dram_tensor("gnw", [D], F32, kind="ExternalInput").ap()
    gup_in = nc.dram_tensor("gup", [256, 2 * I_FFN], I32, kind="ExternalInput").ap()
    gsc_in = nc.dram_tensor("gsc", [256, 2 * I_FFN], FP16, kind="ExternalInput").ap()
    dpk_in = nc.dram_tensor("dpk", [704, D], I32, kind="ExternalInput").ap()
    dsc_in = nc.dram_tensor("dsc", [704, D], FP16, kind="ExternalInput").ap()
    ident_in = nc.dram_tensor("ident", [128, 128], BF16, kind="ExternalInput").ap()
    identf_in = nc.dram_tensor("identf", [16, 16], F32, kind="ExternalInput").ap()
    oneh_in = nc.dram_tensor("oneh", [8, 1], BF16, kind="ExternalInput").ap()
    out_t = nc.dram_tensor("out", [TK, D], F32, kind="ExternalOutput").ap()

    dbg = "ExternalOutput" if debug else "Internal"
    sdt = U16 if debug else BF16
    _bfview = (lambda a: a.bitcast(BF16)) if debug else (lambda a: a)
    gu_scr = _bfview(nc.dram_tensor("gu_scr", [22, 128, 16, 512], sdt, kind=dbg).ap())
    dn_scr = _bfview(nc.dram_tensor("dn_scr", [16, 128, NKI, 128], sdt, kind=dbg).ap())
    wo_scr = nc.dram_tensor("wo_scr", [16, 128, D], BF16, kind="Internal").ap()
    wifg_scr = nc.dram_tensor("wifg_scr", [3, 16, 128, D], BF16, kind="Internal").ap()
    h_scr = _bfview(nc.dram_tensor("h_scr", [D, TK], sdt, kind=dbg).ap())
    a_scr = _bfview(nc.dram_tensor("a_scr", [D, TK], sdt, kind=dbg).ap())
    gl_scr = _bfview(nc.dram_tensor("gl_scr", [D, TK], sdt, kind=dbg).ap())
    hact_scr = _bfview(nc.dram_tensor("hact_scr", [I_FFN, TK], sdt, kind=dbg).ap())
    ffn_scr = _bfview(nc.dram_tensor("ffn_scr", [D, TK], sdt, kind=dbg).ap())
    rstdf_scr = nc.dram_tensor("rstdf_scr", [NTB, TB], F32, kind=dbg).ap()
    xn_dump = stat_dump = None
    if debug:
        xn_dump = nc.dram_tensor("xn_dump", [128, 65536], U16, kind="ExternalOutput").ap()
        stat_dump = nc.dram_tensor("stat_dump", [128, 192], F32, kind="ExternalOutput").ap()
    cc_in = nc.dram_tensor("cc_in", [128, 16], F32, kind="Internal").ap()
    cc_out = nc.dram_tensor("cc_out", [8 * 128, 16], F32, kind="Internal").ap()

    P = Prog()
    from contextlib import ExitStack
    with ExitStack() as es:
        ten = lambda *a: es.enter_context(nc.sbuf_tensor(*a))
        XN = ten("XN", [128, 131072], I8)
        U = ten("U", [128, 53248], I8)
        W = ten("W", [128, 21504], I8)
        ident = ten("s_ident", [128, 128], BF16)
        ones1 = ten("ones1", [1, 128], F32)
        onesb = ten("onesb", [128, 1], BF16)
        gnw = ten("s_gnw", [128, 16], F32)
        oneh = ten("s_oneh", [8, 1], BF16)
        sfin = ten("sfin", [128, 16], F32)
        carry = ten("carry", [128, 16], F32)
        smal = ten("smal", [128, 16], F32)
        srow = ten("srow", [1, 1024], F32)

        pb = [es.enter_context(nc.psum_tensor(f"pb{i}", [128, 512], F32))
              for i in range(8)]
        sems = {s: es.enter_context(nc.semaphore(s)) for s in Prog.SEMS}
        block = es.enter_context(nc.Block())

        bfv = lambda t, off, n: t[:, off:off + 2 * n].bitcast(BF16)
        f32v = lambda t, off, n: t[:, off:off + 4 * n].bitcast(F32)
        i32v = lambda t, off, n: t[:, off:off + 4 * n].bitcast(I32)
        u16v = lambda t, off, n: t[:, off:off + 2 * n].bitcast(U16)
        fp16v = lambda t, off, n: t[:, off:off + 2 * n].bitcast(FP16)

        XNbf = XN[:, :].bitcast(BF16)

        e_id = P.op("GD", lambda e: e.dma_start(ident[:], ident_in), sem="GD")
        e_oh = P.op("GD", lambda e: e.dma_start(oneh[:], oneh_in), sem="GD")
        gnw_r = ten("gnw_r", [16, 128], F32)
        identf = ten("s_identf", [16, 16], F32)
        e_gnl = P.op("GD", lambda e: e.dma_start(
            gnw_r[:], gnw_in.rearrange("(n p) -> n p", n=16)), sem="GD")
        e_idf = P.op("GD", lambda e: e.dma_start(identf[:], identf_in), sem="GD")
        e_gnt = P.op("PE", lambda e: e.transpose(pb[0][:, 0:16], gnw_r[:], identf[:]),
                     waits=[e_gnl, e_idf], sem="PE")
        e_gn = P.op("DV", lambda e: e.tensor_copy(gnw[:], pb[0][:, 0:16]),
                    waits=[e_gnt], sem="DV")
        e_c1 = P.op("DV", lambda e: e.memset(ones1[:], 1.0), sem="DV")
        e_c2 = P.op("DV", lambda e: e.memset(onesb[:], 1.0), sem="DV")
        e_cz = P.op("DV", lambda e: e.memset(carry[:], 0.0), sem="DV")

        def newton_rsqrt(q0, q1, tmp, scale_mul, eps, src, waits):
            P.op("DV", (lambda: lambda e: e.tensor_scalar(
                q0, src, scale_mul, eps, op0=ALU.mult, op1=ALU.add))(), waits=waits)
            P.op("DV", lambda e: e.tensor_scalar(q1.bitcast(I32), q0.bitcast(I32), 1, None,
                                                 op0=ALU.logical_shift_right))
            P.op("DV", lambda e: e.tensor_scalar(q1.bitcast(I32), q1.bitcast(I32), -1, MAGIC,
                                                 op0=ALU.mult, op1=ALU.add))
            ev = None
            for it in range(2):
                P.op("DV", lambda e: e.tensor_tensor(tmp, q1, q1, op=ALU.mult))
                P.op("DV", lambda e: e.tensor_tensor(tmp, tmp, q0, op=ALU.mult))
                P.op("DV", lambda e: e.tensor_scalar(tmp, tmp, -0.5, 1.5, op0=ALU.mult, op1=ALU.add))
                ev = P.op("DV", lambda e: e.tensor_tensor(q1, q1, tmp, op=ALU.mult),
                          sem="DV" if it == 1 else None)
            return ev

        CVF0, CVF1, CVB = 32768, 40960, 49152
        wo_prev = None
        wifg_done = {}
        conv_cv = [None, None]
        conv_st = [None]
        conv_srcs = [(w, n, s) for n in range(16) for w, s in ((0, wi_in), (1, wf_in), (2, wg_in))]
        conv_srcs += [(3, n, wo_in) for n in range(16)]
        conv_i = [0]

        conv_st = [None, None]

        def conv_step(k=1):
            for _ in range(k):
                if conv_i[0] >= len(conv_srcs):
                    return
                idx = conv_i[0]
                conv_i[0] += 1
                wj, n, wsrc = conv_srcs[idx]
                par = idx % 2
                f32t = f32v(U, CVF0, 2048)
                bft = bfv(U, CVB, 2048) if par == 0 else bfv(W, 17408, 2048)
                le = P.op("GD", (lambda d_, s_, n_: lambda e: e.dma_start(
                    d_, s_[n_].rearrange("p k j -> p (k j)").bitcast(F32)))(f32t, wsrc, n),
                    waits=[conv_cv[0]], sem="GD")
                cv = P.op("GD", (lambda d_, s_: lambda e: e.tensor_copy(d_, s_))(bft, f32t),
                          waits=[le, conv_st[par]], sem="CC")
                conv_cv[0] = cv
                dst = wo_scr[n] if wj == 3 else wifg_scr[wj, n]
                se = P.op("SD", (lambda d_, s_: lambda e: e.dma_start(d_, s_))(dst, bft),
                          waits=[cv], sem="SD")
                conv_st[par] = se
                wifg_done[(n, wj) if wj != 3 else ("wo", n)] = se


        statx = ten("statx", [128, 192], F32)
        CS, CQ, CM, CV, CR, CT = 0, 32, 64, 96, 128, 160
        s0a_rd = [None] * 32
        s0a_sq = [None] * 32
        s0a_lx = [None] * 32
        s0_xn = [None] * 32
        s0_pe = [None] * 32
        s0_ev = [None] * 32
        s0c_lx = [None] * 32
        r_evts = [None] * 4
        XNT = (40960, 45056)

        def s0a_iter(i):
            xtm = f32v(U, (0, 8192)[i % 2], 2048)
            sq = f32v(U, (16384, 24576)[i % 2], 1024)
            sqb = f32v(U, (16384, 24576)[i % 2] + 4096, 1024)
            lx = P.op("SD", (lambda x_, t_: lambda e: e.dma_start(
                x_, x_in[t_:t_ + 128, :]))(xtm, i * 128),
                waits=[s0a_rd[i - 2]] if i >= 2 else (), sem="SD")
            s0a_lx[i] = lx
            e_sq = P.op("AC", (lambda x_, s_, s2_: lambda e: (
                e.activation(s_, x_[:, 0:1024], AF.Square),
                e.activation(s2_, x_[:, 1024:2048], AF.Square))[-1])(xtm, sq, sqb),
                waits=[lx, s0a_rd[i - 2] if i >= 2 else None], sem="AC")
            s0a_sq[i] = e_sq
            P.op("DV", (lambda x_, i_: lambda e: e.reduce_sum(
                statx[:, CS + i_:CS + i_ + 1], x_, axis=mybir.AxisListType.X))(xtm, i),
                waits=[lx])
            P.op("DV", (lambda s_, i_: lambda e: e.reduce_sum(
                statx[:, CQ + i_:CQ + i_ + 1], s_, axis=mybir.AxisListType.X))(sq, i),
                waits=[e_sq])
            s0a_rd[i] = P.op("DV", (lambda s_, i_: lambda e: e.reduce_sum(
                statx[:, CT + i_:CT + i_ + 1], s_, axis=mybir.AxisListType.X))(sqb, i),
                sem="DV")
            conv_step(2)

        def sp():
            P.op("DV", lambda e: e.tensor_copy(pb[2][:, 0:256], pb[3][:, 0:256]))

        def s0b_chain(b):
            o = 8 * b
            S_ = statx[:, CS + o:CS + o + 8]
            Q_ = statx[:, CQ + o:CQ + o + 8]
            M_ = statx[:, CM + o:CM + o + 8]
            V_ = statx[:, CV + o:CV + o + 8]
            R_ = statx[:, CR + o:CR + o + 8]
            T_ = statx[:, CT + o:CT + o + 8]
            sp()
            P.op("DV", (lambda Q_, T_: lambda e: e.tensor_tensor(Q_, Q_, T_, op=ALU.add))(Q_, T_),
                 waits=[s0a_rd[o + 7], s0a_sq[o + 7]])
            sp()
            P.op("DV", (lambda M_, S_: lambda e: e.tensor_scalar(
                M_, S_, 1.0 / D, None, op0=ALU.mult))(M_, S_))
            sp()
            P.op("DV", (lambda V_, Q_: lambda e: e.tensor_scalar(
                V_, Q_, 1.0 / D, LN_EPS, op0=ALU.mult, op1=ALU.add))(V_, Q_))
            sp()
            P.op("DV", (lambda T_, M_: lambda e: e.tensor_tensor(T_, M_, M_, op=ALU.mult))(T_, M_))
            sp()
            P.op("DV", (lambda V_, T_: lambda e: e.tensor_sub(V_, V_, T_))(V_, T_))
            sp()
            P.op("DV", (lambda R_, V_: lambda e: e.tensor_scalar(
                R_.bitcast(I32), V_.bitcast(I32), 1, None, op0=ALU.logical_shift_right))(R_, V_))
            sp()
            P.op("DV", (lambda R_: lambda e: e.tensor_scalar(
                R_.bitcast(I32), R_.bitcast(I32), -1, MAGIC, op0=ALU.mult, op1=ALU.add))(R_))
            sp()
            for it in range(2):
                P.op("DV", (lambda T_, R_: lambda e: e.tensor_tensor(T_, R_, R_, op=ALU.mult))(T_, R_))
                sp()
                P.op("DV", (lambda T_, V_: lambda e: e.tensor_tensor(T_, T_, V_, op=ALU.mult))(T_, V_))
                sp()
                P.op("DV", (lambda T_: lambda e: e.tensor_scalar(
                    T_, T_, -0.5, 1.5, op0=ALU.mult, op1=ALU.add))(T_))
                sp()
                P.op("DV", (lambda R_, T_: lambda e: e.tensor_tensor(R_, R_, T_, op=ALU.mult))(R_, T_))
                sp()
            r_evts[b] = P.op("DV", lambda e: e.tensor_copy(pb[2][:, 0:256], pb[3][:, 0:256]),
                             sem="DV")

        def s0c_iter(i):
            b = i // 8
            xtc = f32v(W, (0, 8192)[i % 2], 2048)
            xnt = bfv(U, XNT[i % 2], 2048)
            t0 = i * 128
            lxc = P.op("SD", (lambda x_, t_: lambda e: e.dma_start(
                x_, x_in[t_:t_ + 128, :]))(xtc, t0),
                waits=[s0_xn[i - 2]] if i >= 2 else (), sem="SD")
            s0c_lx[i] = lxc
            s0_xn[i] = P.op("DV", (lambda x_, xn_, i_: lambda e: e.tensor_scalar(
                xn_, x_, statx[:, CM + i_:CM + i_ + 1], statx[:, CR + i_:CR + i_ + 1],
                op0=ALU.subtract, op1=ALU.mult))(xtc, xnt, i),
                waits=[lxc, r_evts[b]] + ([s0_pe[i - 2]] if i >= 2 else []), sem="DV")
            par = i % 2
            bA, bB = pb[4 + par * 2], pb[5 + par * 2]
            for k in range(16):
                c, j = k // 8, k % 8
                srcv = xnt[:, 1024 * c + j: 1024 * c + j + 1017:8]
                dst = (bA if k < 8 else bB)[:].bitcast(BF16)[:, (k % 8) * 128:(k % 8) * 128 + 128]
                ev = P.op("PE", (lambda d_, s_: lambda e: e.transpose(d_, s_, ident[:]))(dst, srcv),
                          waits=[s0_xn[i], e_id, s0_ev[i - 2] if i >= 2 else None],
                          sem="PE" if k == 15 else None)
            s0_pe[i] = ev
            for half in range(2):
                bank = (bA, bB)[half]
                srcv = bank[:].bitcast(BF16).rearrange("p (k t) -> p k t", k=8)
                dstv = XNbf.rearrange("p (k t) -> p k t", k=16)[:, half * 8:half * 8 + 8, t0:t0 + 128]
                ev = P.op("DV", (lambda d_, s_: lambda e: e.tensor_copy(d_, s_))(dstv, srcv),
                          waits=[s0_pe[i]], sem="DV" if half == 1 else None)
            s0_ev[i] = ev

        for k in range(8):
            s0a_iter(k)
        s0b_chain(0)
        for b in range(4):
            for k in range(8):
                s0c_iter(8 * b + k)
                if b + 1 < 4:
                    s0a_iter(8 * (b + 1) + k)
            if b + 1 < 4:
                s0b_chain(b + 1)
        conv_step(10 ** 6)
        wo_scr_done = conv_st[(len(conv_srcs) - 1) % 2]
        xn_ready = s0_ev[31]
        if debug:
            P.op("SD", lambda e: e.dma_start(xn_dump, XNbf.bitcast(U16)),
                 waits=[xn_ready], sem="SD")
            P.op("SD", lambda e: e.dma_start(stat_dump, statx[:, 0:192]),
                 waits=[xn_ready], sem="SD")

        PK = (0, 2048)
        SC = (4096, 5120)
        LOW, FB, GB, SB, BITS = 6144, 8192, 10240, 12288, 14336
        VAL = (15360, 16384)
        wq_state = {"par": 0, "vpar": 0, "valdma": [None, None], "valevt": None,
                    "lastread": [s0_xn[31], s0_xn[31]]}
        wq_done_gu = {}
        wq_done_dn = {}
        wq_queue = []

        def gen_dequant_planes(kind, c, cb):
            size = 64 if (kind == "dn" and c == 5) else 128
            par = wq_state["par"]
            wq_state["par"] ^= 1
            pk = i32v(W, PK[par], 512)[0:size, :]
            sc = fp16v(W, SC[par], 512)[0:size, :]
            if kind == "gu":
                src_pk = gup_in[128 * c:128 * c + size, 512 * cb:512 * cb + 512]
                src_sc = gsc_in[128 * c:128 * c + size, 512 * cb:512 * cb + 512]
            else:
                src_pk = dpk_in[128 * c:128 * c + size, 512 * cb:512 * cb + 512]
                src_sc = dsc_in[128 * c:128 * c + size, 512 * cb:512 * cb + 512]
            wpk = [wq_state["lastread"][par]]
            e_pk = P.op("GD", (lambda d_, s_: lambda e: e.dma_start(d_, s_))(pk, src_pk),
                        waits=wpk, sem="GD")
            e_sc = P.op("GD", (lambda d_, s_: lambda e: e.dma_start(d_, s_))(sc, src_sc),
                        waits=wpk, sem="GD")
            for j in range(8):
                sh = 4 * j
                low = i32v(W, LOW, 512)[0:size, :]
                fb = i32v(W, FB, 512)[0:size, :]
                gb = i32v(W, GB, 512)[0:size, :]
                sb = i32v(W, SB, 512)[0:size, :]
                bits = u16v(W, BITS, 512)[0:size, :]
                vpar = wq_state["vpar"]
                wq_state["vpar"] ^= 1
                val = bfv(W, VAL[vpar], 512)[0:size, :]
                P.op("DV", (lambda o_, i_, s_: lambda e: e.tensor_scalar(
                    o_, i_, s_, 7, op0=ALU.logical_shift_right, op1=ALU.bitwise_and))(low, pk, sh),
                    waits=[e_pk, e_sc, wq_state["valevt"]])
                P.op("DV", (lambda o_, i_: lambda e: e.tensor_scalar(
                    o_, i_, 64, 16128, op0=ALU.mult, op1=ALU.add))(fb, low))
                P.op("DV", (lambda o_, i_, f_: lambda e: e.scalar_tensor_tensor(
                    o_, i_, 16128, f_, op0=ALU.mult, op1=ALU.min))(gb, low, fb))
                if sh <= 12:
                    P.op("DV", (lambda o_, i_, s_: lambda e: e.tensor_scalar(
                        o_, i_, 12 - s_, 0x8000, op0=ALU.logical_shift_left,
                        op1=ALU.bitwise_and))(sb, pk, sh))
                else:
                    P.op("DV", (lambda o_, i_, s_: lambda e: e.tensor_scalar(
                        o_, i_, s_ - 12, 0x8000, op0=ALU.logical_shift_right,
                        op1=ALU.bitwise_and))(sb, pk, sh))
                P.op("DV", (lambda o_, a_, b_: lambda e: e.tensor_tensor(
                    o_, a_, b_, op=ALU.add))(bits, gb, sb))
                vev = P.op("DV", (lambda o_, b_, s_: lambda e: e.tensor_tensor(
                    o_, b_.bitcast(BF16), s_, op=ALU.mult))(val, bits, sc),
                    waits=[wq_state["valdma"][vpar]], sem="DV")
                wq_state["valevt"] = vev
                if j == 7:
                    wq_state["lastread"][par] = vev
                if kind == "gu":
                    dst = gu_scr[cb, 0:size, c * 8 + j, :]
                    sev = P.op("GD", (lambda d_, v_: lambda e: e.dma_start(d_, v_))(dst, val),
                               waits=[vev], sem="GD")
                    wq_state["valdma"][vpar] = sev
                    wq_done_gu.setdefault(cb, []).append(sev)
                else:
                    if c < 5:
                        kf, ro = 8 * c + j, 0
                    else:
                        kf, ro = 40 + j // 2, 64 * (j % 2)
                    for hh in range(4):
                        dst = dn_scr[4 * cb + hh, ro:ro + size, kf, :]
                        sev = P.op("GD", (lambda d_, v_, h_: lambda e: e.dma_start(
                            d_, v_[:, h_ * 128:h_ * 128 + 128]))(dst, val, hh),
                            waits=[vev], sem="GD")
                        wq_state["valdma"][vpar] = sev
                        wq_done_dn.setdefault(4 * cb + hh, []).append(sev)
                yield

        wq_gen = [None]

        def drain_wq(nplanes):
            while nplanes > 0:
                if wq_gen[0] is None:
                    if not wq_queue:
                        return
                    kind, c, cb = wq_queue.pop(0)
                    wq_gen[0] = gen_dequant_planes(kind, c, cb)
                try:
                    next(wq_gen[0])
                    nplanes -= 1
                except StopIteration:
                    wq_gen[0] = None

        wq_queue.extend([("gu", c, cb) for cb in range(22) for c in range(2)])

        WSL = [0, 4096, 8192, 12288, 16384, 20480]
        SI, SN, AFO, SG, IT1, IT2 = 24576, 26624, 28672, 30720, 32768, 34816
        H_ = (36864, 38912)
        A_ = (40960, 43008)
        GL = (45056, 46080)
        HB_ = (47104, 48128)
        AB_ = (49152, 50176)
        wload = {}
        slot_mm = {}
        psum_reader = {}
        h_dma = [None, None]
        a_dma = [None, None]
        g_dma = [None, None]
        sfin_evts = []
        prev = {"it1": None, "it2": None, "dh": None, "da": None, "gl": None}
        s2_entry = [s0_ev[31], s0a_sq[31], s0a_lx[31], s0c_lx[31], r_evts[3], wo_scr_done]

        for n in range(16):
            for wi_ in (0, 1, 2):
                slot = (n % 2) * 3 + wi_
                wbuf = bfv(U, WSL[slot], 2048)
                wload[(n, wi_)] = P.op("SD", (lambda d_, w_, n_: lambda e: e.dma_start(
                    d_, wifg_scr[w_, n_]))(wbuf, wi_, n),
                    waits=[slot_mm.get(slot), wifg_done[(n, wi_)]]
                          + (s2_entry if n < 2 else []), sem="SD")
            for tb in range(NTB):
                par = tb % 2
                bi, bfp, bg = pb[par], pb[2 + par], pb[4 + par]
                mm = []
                for wi_, bank in ((0, bi), (1, bfp), (2, bg)):
                    slot = (n % 2) * 3 + wi_
                    wbuf = bfv(U, WSL[slot], 2048)
                    for k in range(NK):
                        ev = P.op("PE", (lambda o_, l_, r_, st_, sp_: lambda e: e.matmul(
                            o_, l_, r_, start=st_, stop=sp_))(
                            bank[:], wbuf[:, k * 128:(k + 1) * 128],
                            XNbf[:, k * 4096 + tb * TB: k * 4096 + tb * TB + TB],
                            k == 0, k == NK - 1),
                            waits=[wload[(n, wi_)], xn_ready,
                                   psum_reader.get(id(bank))] if k == 0 else (),
                            sem="PE" if k == NK - 1 else None)
                    mm.append(ev)
                    slot_mm[slot] = ev
                e_i, e_f, e_g = mm
                si = f32v(U, SI, 512)
                sn = f32v(U, SN, 512)
                af = f32v(U, AFO, 512)
                sg = f32v(U, SG, 512)
                it1 = f32v(U, IT1, 512)
                it2 = f32v(U, IT2, 512)
                a1 = P.op("AC", (lambda o_, b_: lambda e: e.activation(
                    o_, b_[:], AF.Sigmoid))(si, bi), waits=[e_i, prev["it1"]], sem="AC")
                a2 = P.op("AC", (lambda o_, b_: lambda e: e.activation(
                    o_, b_[:], AF.Sigmoid, scale=-1.0))(sn, bfp), waits=[e_f, prev["it2"]], sem="AC")
                a3 = P.op("AC", (lambda o_, b_: lambda e: e.activation(
                    o_, b_[:], AF.Sigmoid))(af, bfp), waits=[e_f, prev["da"]], sem="AC")
                a4 = P.op("AC", (lambda o_, b_: lambda e: e.activation(
                    o_, b_[:], AF.Sigmoid))(sg, bg), waits=[e_g, prev["gl"]], sem="AC")
                d_it1 = P.op("DV", (lambda o_, s_, b_: lambda e: e.tensor_tensor(
                    o_, s_, b_[:], op=ALU.mult))(it1, si, bi), waits=[a1], sem="DV")
                psum_reader[id(bi)] = d_it1
                d_it2 = P.op("DV", (lambda o_, a_, s_: lambda e: e.tensor_tensor(
                    o_, a_, s_, op=ALU.mult))(it2, it1, sn), waits=[a2], sem="DV")
                prev["it1"], prev["it2"] = d_it1, d_it2
                hT = f32v(U, H_[par], 512)
                aT = f32v(U, A_[par], 512)
                hprev = f32v(U, H_[1 - par], 512)
                aprev = f32v(U, A_[1 - par], 512)
                hinit = 0.0 if tb == 0 else hprev[:, TB - 1:TB]
                ainit = 1.0 if tb == 0 else aprev[:, TB - 1:TB]
                dh = P.op("DV", (lambda o_, a_, b_, i_: lambda e: e.tensor_tensor_scan(
                    o_, a_, b_, i_, op0=ALU.mult, op1=ALU.add))(hT, af, it2, hinit),
                    waits=[a3, h_dma[par]], sem="DV")
                da = P.op("DV", (lambda o_, a_, i_: lambda e: e.tensor_tensor_scan(
                    o_, a_, a_, i_, op0=ALU.mult, op1=ALU.min))(aT, af, ainit),
                    waits=[a_dma[par]], sem="DV")
                prev["dh"], prev["da"] = dh, da
                gl = bfv(U, GL[par], 512)
                dgl = P.op("DV", (lambda o_, s_, b_: lambda e: e.tensor_tensor(
                    o_, s_, b_[:], op=ALU.mult))(gl, sg, bg), waits=[a4, g_dma[par]], sem="DV")
                prev["gl"] = dgl
                psum_reader[id(bg)] = dgl
                psum_reader[id(bfp)] = a3
                sfw = []
                if tb == NTB - 1:
                    sf = P.op("DV", (lambda h_, n_: lambda e: e.tensor_copy(
                        sfin[:, n_:n_ + 1], h_[:, TB - 1:TB]))(hT, n), sem="DV")
                    sfin_evts.append(sf)
                    sfw = [sf]
                hbf = bfv(U, HB_[par], 512)
                abf = bfv(U, AB_[par], 512)
                cvh = P.op("DV", (lambda d_, s_: lambda e: e.tensor_copy(d_, s_))(hbf, hT),
                           waits=[h_dma[par]], sem="DV")
                cva = P.op("DV", (lambda d_, s_: lambda e: e.tensor_copy(d_, s_))(abf, aT),
                           waits=[a_dma[par]], sem="DV")
                h_dma[par] = P.op("GD", (lambda h_, n_, t_: lambda e: e.dma_start(
                    h_scr[128 * n_:128 * n_ + 128, TB * t_:TB * t_ + TB], h_))(hbf, n, tb),
                    waits=[cvh], sem="GD")
                a_dma[par] = P.op("GD", (lambda a_, n_, t_: lambda e: e.dma_start(
                    a_scr[128 * n_:128 * n_ + 128, TB * t_:TB * t_ + TB], a_))(abf, n, tb),
                    waits=[cva], sem="GD")
                g_dma[par] = P.op("GD", (lambda g_, n_, t_: lambda e: e.dma_start(
                    gl_scr[128 * n_:128 * n_ + 128, TB * t_:TB * t_ + TB], g_))(gl, n, tb),
                    waits=[dgl], sem="GD")
                drain_wq(3)
        s2_done_dmas = [h_dma[0], h_dma[1], a_dma[0], a_dma[1], g_dma[0], g_dma[1]]

        e_ccin = P.op("GD", lambda e: e.dma_start(cc_in, sfin[:]), waits=sfin_evts, sem="GD")
        e_cc = P.op("GD", lambda e: e.collective_compute(
            "AllGather", ALU.bypass, replica_groups=[[0, 1, 2, 3, 4, 5, 6, 7]],
            ins=[cc_in.opt()], outs=[cc_out.opt()]), waits=[e_ccin], sem="CC")
        sagb = W[0:8, 17408:21504].bitcast(BF16)
        e_sag = P.op("GD", lambda e: e.dma_start(
            sagb, cc_out.rearrange("(r p) n -> r (p n)", r=8)),
            waits=[e_cc, wo_scr_done], sem="GD")

        drain_wq(10 ** 6)
        wq_queue.extend([("dn", c, cb) for cb in range(4) for c in range(6)])

        GUS = [0, 16384, 32768]
        SIL, T1 = 49152, 50176
        HA = (51200, 52224)
        s3a_entry = P.barrier()
        gu_load = {}
        gu_slot_mm = {}
        hact_dma = {}
        first_slot_use = set()
        prev_sil = None
        prev_t1 = None
        s4_done = False
        carry_evt = None
        last_ha_evt = None

        def gu_block_load(bi_, slot):
            w = [gu_slot_mm.get(slot)] + wq_done_gu.get(bi_, [])
            if slot not in first_slot_use:
                first_slot_use.add(slot)
                w += s3a_entry
            dst = bfv(U, GUS[slot], 8192)
            return P.op("SD", (lambda d_, b_: lambda e: e.dma_start(
                d_, gu_scr[b_].rearrange("p k j -> p (k j)")))(dst, bi_),
                waits=w, sem="SD")

        gu_load[0] = gu_block_load(0, 0)
        gu_load[11] = gu_block_load(11, 1)
        slot_of = {0: 0, 11: 1}
        next_slot = 2
        for gb in range(11):
            if gb + 1 < 11:
                gu_load[gb + 1] = gu_block_load(gb + 1, next_slot)
                slot_of[gb + 1] = next_slot
            gslot, uslot = slot_of[gb], slot_of[gb + 11]
            gt = bfv(U, GUS[gslot], 8192)
            up = bfv(U, GUS[uslot], 8192)
            for oc in range(4):
                ci = 4 * gb + oc
                for tb in range(NTB):
                    par = tb % 2
                    bgt, bup = pb[par], pb[2 + par]
                    for k in range(NK):
                        eg = P.op("PE", (lambda o_, l_, r_, st_, sp_: lambda e: e.matmul(
                            o_, l_, r_, start=st_, stop=sp_))(
                            bgt[:], gt[:, k * 512 + oc * 128: k * 512 + oc * 128 + 128],
                            XNbf[:, k * 4096 + tb * TB: k * 4096 + tb * TB + TB],
                            k == 0, k == NK - 1),
                            waits=[gu_load[gb], psum_reader.get(id(bgt))] if k == 0 else (),
                            sem="PE" if k == NK - 1 else None)
                    for k in range(NK):
                        eu = P.op("PE", (lambda o_, l_, r_, st_, sp_: lambda e: e.matmul(
                            o_, l_, r_, start=st_, stop=sp_))(
                            bup[:], up[:, k * 512 + oc * 128: k * 512 + oc * 128 + 128],
                            XNbf[:, k * 4096 + tb * TB: k * 4096 + tb * TB + TB],
                            k == 0, k == NK - 1),
                            waits=[gu_load[gb + 11], psum_reader.get(id(bup))] if k == 0 else (),
                            sem="PE" if k == NK - 1 else None)
                    gu_slot_mm[gslot] = eg
                    gu_slot_mm[uslot] = eu
                    sil = bfv(U, SIL, 512)
                    asl = P.op("AC", (lambda o_, b_: lambda e: e.activation(
                        o_, b_[:], AF.Sigmoid))(sil, bgt),
                        waits=[eg, prev_t1, wo_scr_done], sem="AC")
                    t1 = bfv(U, T1, 512)
                    dt1 = P.op("DV", (lambda o_, s_, b_: lambda e: e.tensor_tensor(
                        o_, s_, b_[:], op=ALU.mult))(t1, sil, bgt), waits=[asl], sem="DV")
                    prev_t1 = dt1
                    psum_reader[id(bgt)] = dt1
                    ha = bfv(U, HA[par], 512)
                    dha = P.op("DV", (lambda o_, t_, b_: lambda e: e.tensor_tensor(
                        o_, t_, b_[:], op=ALU.mult))(ha, t1, bup),
                        waits=[eu, hact_dma.get(par)], sem="DV")
                    psum_reader[id(bup)] = dha
                    hact_dma[par] = P.op("GD", (lambda h_, c_, t_: lambda e: e.dma_start(
                        hact_scr[128 * c_:128 * c_ + 128, TB * t_:TB * t_ + TB], h_))(ha, ci, tb),
                        waits=[dha], sem="GD")
                    drain_wq(1)
                drain_wq(2)
            if gb + 12 < 22:
                free = slot_of[gb]
                gu_load[gb + 12] = gu_block_load(gb + 12, free)
                slot_of[gb + 12] = free
                next_slot = slot_of[gb + 11]
                del slot_of[gb]
            if gb == 2 and not s4_done:
                s4_done = True
                for nn in range(16):
                    evp = P.op("PE", (lambda nn_: lambda e: e.matmul(
                        pb[7][:, nn_:nn_ + 1], sagb[:, nn_::16], oneh[:],
                        start=True, stop=True))(nn),
                        waits=[e_sag, e_oh], sem="PE" if nn == 15 else None)
                carry_evt = P.op("DV", lambda e: e.tensor_copy(carry[:], pb[7][:, 0:16]),
                                 waits=[evp, e_cz], sem="DV")
                psum_reader[id(pb[7])] = carry_evt
        drain_wq(10 ** 6)
        s3a_done = [hact_dma[0], hact_dma[1]]
        gu_last_pe = [gu_slot_mm[s] for s in gu_slot_mm]

        s3b_entry = P.barrier()
        DN = (0, 11264)
        HAC = (22528, 67584)
        dn_slot_mm = [None, None]
        y2_rd = [None, None]
        hac_last = [None, None]
        ffn_dma = {}
        stats_rd = None
        rstdf_gd = None
        q0, q1 = srow[:, 0:512], srow[:, 512:1024]
        tmp_row = f32v(U, 43008, 512)[0:1, :]
        hl = [None, None]

        def hac_load(t_):
            par = t_ % 2
            return P.op("SD", (lambda d_, tt_: lambda e: e.dma_start(
                d_, hact_scr.rearrange("(k p) t -> p k t", p=128)[:, :, TB * tt_:TB * tt_ + TB]))(
                bfv(XN, HAC[par], 22528), t_),
                waits=(s3b_entry if t_ == 0 else []) + [hac_last[par]] + s3a_done, sem="SD")

        hl[0] = hac_load(0)
        for tb in range(NTB):
            hac = bfv(XN, HAC[tb % 2], 22528)
            stats = pb[6]
            est = None
            for m in range(16):
                slot = m % 2
                dnb = bfv(XN, DN[slot], 5632)
                dl = P.op("SD", (lambda d_, o_: lambda e: e.dma_start(
                    d_, dn_scr[o_].rearrange("p k j -> p (k j)")))(dnb, m),
                    waits=(s3b_entry if tb == 0 else []) + [dn_slot_mm[slot]]
                          + (wq_done_dn.get(m, []) if tb == 0 else []), sem="SD")
                par = m % 2
                by = pb[par]
                for k in range(NKI):
                    ey = P.op("PE", (lambda o_, l_, r_, st_, sp_: lambda e: e.matmul(
                        o_, l_, r_, start=st_, stop=sp_))(
                        by[:], dnb[:, k * 128:(k + 1) * 128],
                        hac[:, k * TB: k * TB + TB], k == 0, k == NKI - 1),
                        waits=[dl, hl[tb % 2], psum_reader.get(id(by))] if k == 0 else (),
                        sem="PE" if k == NKI - 1 else None)
                dn_slot_mm[slot] = ey
                hac_last[tb % 2] = ey
                yb = bfv(U, (49152, 50176)[par], 512)
                y2 = bfv(U, (51200, 52224)[par], 512)
                ac1 = P.op("AC", (lambda o_, b_: lambda e: e.activation(
                    o_, b_[:], AF.Copy))(yb, by), waits=[ey, ffn_dma.get(par)], sem="AC")
                ac2 = P.op("AC", (lambda o_, b_: lambda e: e.activation(
                    o_, b_[:], AF.Square))(y2, by), waits=[ey, y2_rd[par]], sem="AC")
                psum_reader[id(by)] = ac2
                est = P.op("PE", (lambda y_, st_, sp_: lambda e: e.matmul(
                    stats[0:1, 0:TB], onesb[:], y_, start=st_, stop=sp_))(y2, m == 0, m == 15),
                    waits=[ac2, e_c2] + ([stats_rd] if m == 0 else []),
                    sem="PE")
                y2_rd[par] = est
                ffn_dma[par] = P.op("GD", (lambda y_, m_, t_: lambda e: e.dma_start(
                    ffn_scr[128 * m_:128 * m_ + 128, TB * t_:TB * t_ + TB], y_))(yb, m, tb),
                    waits=[ac1], sem="GD")
                if m == 1 and tb + 1 < NTB:
                    hl[(tb + 1) % 2] = hac_load(tb + 1)
            stats_rd = newton_rsqrt(q0, q1, tmp_row, 1.0 / D, FFN_EPS, stats[0:1, 0:TB],
                                    waits=[est, rstdf_gd] + (s3b_entry if tb == 0 else []))
            rstdf_gd = P.op("GD", (lambda t_: lambda e: e.dma_start(
                rstdf_scr[t_:t_ + 1, :], q1))(tb), waits=[stats_rd], sem="GD")
        s3b_ffn_done = [ffn_dma[0], ffn_dma[1]]

        s5_entry = P.barrier()
        HS, AS, GS, FS, HG, OM0, XT = 0, 16384, 32768, 49152, 65536, 81920, 98304
        WOS = (0, 4096)
        OTM = (8192, 16384, 24576, 32768)
        SUM = (40960, 41984)
        SQT = (43008, 44032)
        TMP2, RBH = 45056, 47104
        RBF = (49152, 51200)
        h_sb = bfv(XN, HS, 8192)
        a_sb = bfv(XN, AS, 8192)
        g_sb = bfv(XN, GS, 8192)
        f_sb = bfv(XN, FS, 8192)
        hg_sb = bfv(XN, HG, 8192)
        om_of = lambda t_: bfv(XN, OM0, 8192) if t_ % 2 == 0 else bfv(W, 0, 8192)
        rbh = f32v(U, RBH, 512)
        tmp2 = f32v(U, TMP2, 512)
        rbh_row = rbh[0:1, :]

        st5 = {
            "ld_h": None, "ld_g": None, "ld_f": None, "d1_last": {},
            "aq_last": None, "stats_rd": stats_rd, "sq_rd": [None, None],
            "sm_rd": [None, None], "q_guard": None, "wo_prev": [None, None],
            "out_prev": {}, "om_done": {}, "hg_last": {}, "x_evts": {},
        }
        last_out = [None]

        def s5_loads(tb):
            first = (s5_entry if tb == 0 else []) + (s2_done_dmas if tb == 0 else [])

            def scr_load(dst, scr, extra):
                return P.op("SD", (lambda d_, s_, t_: lambda e: e.dma_start(
                    d_, s_.rearrange("(k p) t -> p k t", p=128)[:, :, TB * t_:TB * t_ + TB]))(dst, scr, tb),
                    waits=extra + first, sem="SD")

            lh = scr_load(h_sb, h_scr, [st5["ld_h"]])
            la = scr_load(a_sb, a_scr, [st5["ld_h"]])
            lg = scr_load(g_sb, gl_scr, [st5["ld_g"]])
            rbf = f32v(U, RBF[tb % 2], 512)
            lr = P.op("GD", (lambda t_, r_: lambda e: e.dma_start(
                r_[0:1, :], rstdf_scr[t_:t_ + 1, :]))(tb, rbf),
                waits=[rstdf_gd, st5["d1_last"].get(tb - 2)] + (s5_entry if tb == 0 else []),
                sem="GD")
            return lh, la, lg, lr

        def s5_hg(tb, lh, la):
            hg_evts = []
            aq = None
            for n in range(16):
                dhg = P.op("DV", (lambda n_: lambda e: e.scalar_tensor_tensor(
                    hg_sb[:, n_ * TB:n_ * TB + TB], a_sb[:, n_ * TB:n_ * TB + TB],
                    carry[:, n_:n_ + 1], h_sb[:, n_ * TB:n_ * TB + TB],
                    op0=ALU.mult, op1=ALU.add))(n),
                    waits=[lh, la, carry_evt, st5["aq_last"]], sem="DV")
                sqt = bfv(U, SQT[n % 2], 512)
                aq = P.op("AC", (lambda n_, s_: lambda e: e.activation(
                    s_, hg_sb[:, n_ * TB:n_ * TB + TB], AF.Square))(n, sqt),
                    waits=[dhg, st5["sq_rd"][n % 2]] + (s5_entry if tb == 0 and n < 2 else []),
                    sem="AC")
                hg_evts.append((dhg, aq))
            st5["aq_last"] = aq
            st5["hg_last"][tb] = hg_evts[15][0]
            return hg_evts

        def s5_stats(tb, hg_evts):
            stats = pb[6]
            est = None
            for n in range(16):
                sqt = bfv(U, SQT[n % 2], 512)
                est = P.op("PE", (lambda s_, st_, sp_: lambda e: e.matmul(
                    stats[0:1, 0:TB], onesb[:], s_, start=st_, stop=sp_))(sqt, n == 0, n == 15),
                    waits=[hg_evts[n][1]] + ([st5["stats_rd"]] if n == 0 else []),
                    sem="PE")
            st5["sq_rd"][0] = est
            st5["sq_rd"][1] = est
            return est

        def s5_rstd(tb, est, lr):
            st5["stats_rd"] = newton_rsqrt(q0, q1, rbh_row, 1.0 / D, RMS_EPS,
                                           pb[6][0:1, 0:TB],
                                           waits=[est, st5["q_guard"],
                                                  Evt("GD", lr.val, "none", True)])
            rbf = f32v(U, RBF[tb % 2], 512)
            eb1 = P.op("PE", lambda e: e.matmul(pb[7][:, 0:TB], ones1[:], q1, start=True, stop=True),
                       waits=[st5["stats_rd"], e_c1, psum_reader.get(id(pb[7]))], sem="PE")
            eb2 = P.op("PE", (lambda r_: lambda e: e.matmul(
                pb[3][:, 0:TB], ones1[:], r_[0:1, 0:TB], start=True, stop=True))(rbf),
                waits=[lr, e_c1, psum_reader.get(id(pb[3]))], sem="PE")
            ebh = P.op("DV", lambda e: e.tensor_copy(rbh, pb[7][:]), waits=[eb1], sem="DV")
            ebf = P.op("DV", (lambda r_: lambda e: e.tensor_copy(r_, pb[3][:]))(rbf),
                       waits=[eb2], sem="DV")
            psum_reader[id(pb[7])] = ebh
            psum_reader[id(pb[3])] = ebf
            st5["q_guard"] = ebh
            return ebh, ebf

        def s5_omat(tb, lg, ebh):
            om_sb = om_of(tb)
            om_done = None
            for n in range(16):
                P.op("DV", (lambda n_: lambda e: e.tensor_tensor(
                    hg_sb[:, n_ * TB:n_ * TB + TB], hg_sb[:, n_ * TB:n_ * TB + TB],
                    g_sb[:, n_ * TB:n_ * TB + TB], op=ALU.mult))(n),
                    waits=[lg, Evt("AC", st5["aq_last"].val, "none", False)])
                om_done = P.op("DV", (lambda n_, o_: lambda e: e.scalar_tensor_tensor(
                    o_[:, n_ * TB:n_ * TB + TB], hg_sb[:, n_ * TB:n_ * TB + TB],
                    gnw[:, n_:n_ + 1], rbh, op0=ALU.mult, op1=ALU.mult))(n, om_sb),
                    waits=[ebh, e_gn] + (s5_entry if tb <= 1 and n == 0 else []),
                    sem="DV" if n == 15 else None)
            st5["om_done"][tb] = om_done
            st5["ld_g"] = om_done

        def s5_prep(tb, upto):
            if upto == 0:
                st5["_l"] = s5_loads(tb)
                st5["_hg"] = s5_hg(tb, st5["_l"][0], st5["_l"][1])
                st5["ld_h"] = st5["hg_last"][tb]
            elif upto == 1:
                st5["_est"] = s5_stats(tb, st5["_hg"])
            else:
                lh, la, lg, lr = st5["_l"]
                ebh, ebf = s5_rstd(tb, st5["_est"], lr)
                st5["_ebf"] = ebf
                s5_omat(tb, lg, ebh)

        def s5_main(tb):
            om_sb = om_of(tb)
            rbf = f32v(U, RBF[tb % 2], 512)
            ebf = st5["_main_ebf"]
            om_done = st5["om_done"][tb]
            lf = st5["_lf"]
            x_evts = []
            for qq in range(4):
                xv = f32v(XN, XT + qq * 8192, 2048)
                x_evts.append(P.op("SD", (lambda x_, t_, q_: lambda e: e.dma_start(
                    x_, x_in[t_ * TB + q_ * 128: t_ * TB + q_ * 128 + 128, :]))(xv, tb, qq),
                    waits=[st5["out_prev"].get(qq)] + (s5_entry if tb == 0 else []), sem="SD"))
            evac_last = None
            d1 = None
            d2 = None
            for m in range(16):
                par = m % 2
                wob = bfv(U, WOS[par], 2048)
                wl2 = P.op("SD", (lambda d_, m_: lambda e: e.dma_start(
                    d_, wo_scr[m_]))(wob, m),
                    waits=[st5["wo_prev"][par], wo_scr_done] +
                          (gu_last_pe + s5_entry if tb == 0 and m < 2 else []),
                    sem="SD")
                ba = pb[par]
                for k in range(NK):
                    ea = P.op("PE", (lambda o_, l_, r_, st_, sp_: lambda e: e.matmul(
                        o_, l_, r_, start=st_, stop=sp_))(
                        ba[:], wob[:, k * 128:(k + 1) * 128],
                        om_sb[:, k * TB:k * TB + TB], k == 0, k == NK - 1),
                        waits=[wl2, om_done, psum_reader.get(id(ba))] if k == 0 else (),
                        sem="PE" if k == NK - 1 else None)
                st5["wo_prev"][par] = ea
                sm = bfv(U, SUM[par], 512)
                d1 = P.op("DV", (lambda m_: lambda e: e.tensor_tensor(
                    tmp2, f_sb[:, m_ * TB:m_ * TB + TB], rbf, op=ALU.mult))(m),
                    waits=[lf, ebf], sem="DV" if m == 15 else None)
                d2 = P.op("DV", (lambda s_, b_: lambda e: e.tensor_tensor(
                    s_, tmp2, b_[:], op=ALU.add))(sm, ba),
                    waits=[ea, st5["sm_rd"][par]], sem="DV")
                psum_reader[id(ba)] = d2
                bt = pb[4 + par]
                for qq in range(4):
                    et = P.op("PE", (lambda d_, s_, q_: lambda e: e.transpose(
                        d_, s_[:, q_ * 128:q_ * 128 + 128], ident[:]))(
                        bt[:].bitcast(BF16)[:, qq * 128:qq * 128 + 128], sm, qq),
                        waits=[d2] + ([psum_reader.get(id(bt))] if qq == 0 else []),
                        sem="PE" if qq == 3 else None)
                st5["sm_rd"][par] = et
                otm4 = f32v(U, OTM[0], 8192).rearrange(
                    "p (q t) -> p q t", q=4)[:, :, m * 128:m * 128 + 128]
                xv4 = f32v(XN, XT, 8192).rearrange(
                    "p (q t) -> p q t", q=4)[:, :, m * 128:m * 128 + 128]
                bt4 = bt[:].bitcast(BF16)[:, 0:512].rearrange("p (q t) -> p q t", q=4)
                ev = P.op("DV", (lambda o_, x_, b_: lambda e: e.tensor_tensor(
                    o_, b_, x_, op=ALU.add))(otm4, xv4, bt4),
                    waits=[et] + x_evts + [st5["out_prev"].get(q) for q in range(4)],
                    sem="DV")
                evac_last = ev
                psum_reader[id(bt)] = ev
                if tb + 1 < NTB:
                    if m == 1:
                        s5_prep(tb + 1, 0)
                    elif m == 6:
                        s5_prep(tb + 1, 1)
                    elif m == 9:
                        s5_prep(tb + 1, 2)
            st5["d1_last"][tb] = d1
            for qq in range(4):
                otm = f32v(U, OTM[qq], 2048)
                oe = P.op("GD", (lambda o_, t_, q_: lambda e: e.dma_start(
                    out_t[t_ * TB + q_ * 128: t_ * TB + q_ * 128 + 128, :], o_))(otm, tb, qq),
                    waits=[evac_last], sem="GD")
                st5["out_prev"][qq] = oe
                last_out[0] = oe
            if tb + 1 < NTB:
                lf2 = P.op("SD", (lambda t_: lambda e: e.dma_start(
                    f_sb.rearrange("p (k t) -> p k t", k=16),
                    ffn_scr.rearrange("(k p) t -> p k t", p=128)[:, :, TB * t_:TB * t_ + TB]))(tb + 1),
                    waits=[d1], sem="SD")
                st5["_lf"] = lf2
                st5["_main_ebf"] = st5["_ebf"]

        st5["_l"] = None
        s5_prep(0, 0)
        s5_prep(0, 1)
        s5_prep(0, 2)
        lf0 = P.op("SD", lambda e: e.dma_start(
            f_sb.rearrange("p (k t) -> p k t", k=16),
            ffn_scr.rearrange("(k p) t -> p k t", p=128)[:, :, 0:TB]),
            waits=s5_entry + s3b_ffn_done, sem="SD")
        st5["_lf"] = lf0
        st5["_main_ebf"] = st5["_ebf"]
        for tb in range(NTB):
            s5_main(tb)
        last_out = last_out[0]

        P.op("SD", lambda e: e.nop(),
             waits=[Evt("SD", P.counts["SD"], "none", True),
                    Evt("GD", P.counts["GD"], "none", True)])

        @block.sync
        def _(s):
            P.emit_engine("SD", s, sems)

        @block.gpsimd
        def _(g):
            P.emit_engine("GD", g, sems)

        @block.vector
        def _(v):
            P.emit_engine("DV", v, sems)

        @block.scalar
        def _(a):
            P.emit_engine("AC", a, sems)

        @block.tensor
        def _(t):
            P.emit_engine("PE", t, sems)

    return nc


def _host_prep(Wi, Wf, Wg, Wo, gup, gsc, dpk, dsc):
    pH = _perm_H()
    pI = _perm_I()
    wi_t = _wtile(Wi[pH])
    wf_t = _wtile(Wf[pH])
    wg_t = _wtile(Wg[pH])
    wo_t = _wtile(Wo)
    rows = np.concatenate([gup[:I_FFN][pI], gup[I_FFN:][pI]], axis=0)
    gup_T = np.ascontiguousarray(rows.T)
    scl = np.concatenate([gsc[:, :I_FFN][:, pI], gsc[:, I_FFN:][:, pI]], axis=1)
    gsc_exp = np.ascontiguousarray(np.repeat(scl, 16, axis=0))
    dpk_T = np.ascontiguousarray(dpk.T)
    dsc_exp = np.ascontiguousarray(np.repeat(dsc, 16, axis=0))
    return wi_t, wf_t, wg_t, wo_t, gup_T, gsc_exp, dpk_T, dsc_exp


def kernel(L0_attn_hidden_states, L0_attn_Wi, L0_attn_Wf, L0_attn_Wg, L0_attn_Wo,
           L0_attn_g_norm_weight, L0_ffn_gate_up_packed, L0_ffn_gate_up_scales,
           L0_ffn_down_packed, L0_ffn_down_scales):
    x = np.asarray(L0_attn_hidden_states, dtype=np.float32)
    wi_t, wf_t, wg_t, wo_t, gup_T, gsc_exp, dpk_T, dsc_exp = _host_prep(
        np.asarray(L0_attn_Wi, np.float32), np.asarray(L0_attn_Wf, np.float32),
        np.asarray(L0_attn_Wg, np.float32), np.asarray(L0_attn_Wo, np.float32),
        np.asarray(L0_ffn_gate_up_packed, np.int32), np.asarray(L0_ffn_gate_up_scales),
        np.asarray(L0_ffn_down_packed, np.int32), np.asarray(L0_ffn_down_scales))
    gnw = np.asarray(L0_attn_g_norm_weight, np.float32)
    ident = np.eye(128, dtype=ml_dtypes.bfloat16)

    nc = build()
    in_maps = []
    for c in range(8):
        b, half = c // 2, c % 2
        oh = np.zeros((8, 1), dtype=ml_dtypes.bfloat16)
        if half == 1:
            oh[c - 1, 0] = 1.0
        in_maps.append(dict(
            x=np.ascontiguousarray(x[b, half * TK:(half + 1) * TK, :]),
            wi=wi_t, wf=wf_t, wg=wg_t, wo=wo_t, gnw=gnw,
            gup=gup_T, gsc=gsc_exp, dpk=dpk_T, dsc=dsc_exp,
            ident=ident, identf=np.eye(16, dtype=np.float32), oneh=oh,
        ))
    res = run_bass_kernel_spmd(nc, in_maps, core_ids=list(range(8)))
    out = np.empty((4, 8192, D), np.float32)
    for c in range(8):
        b, half = c // 2, c % 2
        out[b, half * TK:(half + 1) * TK, :] = np.asarray(res.results[c]["out"])
    return out


# version 14 (final)
# speedup vs baseline: 1.0111x; 1.0111x over previous
import numpy as np
import ml_dtypes
import concourse.bass as bass
import concourse.mybir as mybir
from concourse.bass_utils import run_bass_kernel_spmd

F32 = mybir.dt.float32
BF16 = mybir.dt.bfloat16
FP16 = mybir.dt.float16
I32 = mybir.dt.int32
U16 = mybir.dt.uint16
I8 = mybir.dt.int8
AF = mybir.ActivationFunctionType
ALU = mybir.AluOpType

D = 2048
I_FFN = 5632
TK = 4096
TB = 512
NTB = TK // TB
NK = D // 128
NKI = I_FFN // 128
LN_EPS = 1e-5
RMS_EPS = 1e-5
FFN_EPS = 1e-6
MAGIC = 0x5F3759DF


class Evt:
    __slots__ = ("sem", "val", "eng", "dma")

    def __init__(self, sem, val, eng, dma):
        self.sem, self.val, self.eng, self.dma = sem, val, eng, dma


class Prog:
    SEMS = ("SD", "GD", "PE", "DV", "AC", "CC")

    def __init__(self):
        self.steps = []
        self.counts = {s: 0 for s in self.SEMS}

    def op(self, eng, fn, waits=(), sem=None):
        waits = tuple(w for w in waits if w is not None)
        self.steps.append((eng, fn, waits, sem))
        if sem is None:
            return None
        inc = 16 if sem in ("SD", "GD") else 1
        self.counts[sem] += inc
        return Evt(sem, self.counts[sem], eng, sem in ("SD", "GD", "CC"))

    def barrier(self, sems=("PE", "DV", "AC", "SD", "GD")):
        return [Evt(s, self.counts[s], "none", True) for s in sems
                if self.counts[s] > 0]

    def emit_engine(self, eng_name, eng_obj, handles):
        waited = {}
        for eng, fn, waits, sem in self.steps:
            if eng != eng_name:
                continue
            need = {}
            for w in waits:
                if w.eng == eng_name and not w.dma:
                    continue
                if w.val > need.get(w.sem, 0):
                    need[w.sem] = w.val
            for s, v in need.items():
                if waited.get(s, 0) >= v:
                    continue
                eng_obj.wait_ge(handles[s], v)
                waited[s] = v
            instr = fn(eng_obj)
            if sem is not None:
                inc = 16 if sem in ("SD", "GD") else 1
                instr.then_inc(handles[sem], inc)


def _perm_H():
    kk = np.arange(D)
    kappa, p = kk // 128, kk % 128
    c, j = kappa // 8, kappa % 8
    return 8 * (128 * c + p) + j


def _perm_I():
    out = []
    for c in range(6):
        size = 128 if c < 5 else 64
        for j in range(8):
            for p in range(size):
                out.append(8 * (128 * c + p) + j)
    return np.asarray(out)


def _wtile(W):
    return np.ascontiguousarray(W.reshape(16, 128, 16, 128).transpose(2, 1, 0, 3))


def build(debug=False):
    nc = bass.Bass("TRN2", target_bir_lowering=False)

    x_in = nc.dram_tensor("x", [TK, D], F32, kind="ExternalInput").ap()
    wi_in = nc.dram_tensor("wi", [16, 128, 16, 128], F32, kind="ExternalInput").ap()
    wf_in = nc.dram_tensor("wf", [16, 128, 16, 128], F32, kind="ExternalInput").ap()
    wg_in = nc.dram_tensor("wg", [16, 128, 16, 128], F32, kind="ExternalInput").ap()
    wo_in = nc.dram_tensor("wo", [16, 128, 16, 128], F32, kind="ExternalInput").ap()
    gnw_in = nc.dram_tensor("gnw", [D], F32, kind="ExternalInput").ap()
    gup_in = nc.dram_tensor("gup", [256, 2 * I_FFN], I32, kind="ExternalInput").ap()
    gsc_in = nc.dram_tensor("gsc", [256, 2 * I_FFN], FP16, kind="ExternalInput").ap()
    dpk_in = nc.dram_tensor("dpk", [704, D], I32, kind="ExternalInput").ap()
    dsc_in = nc.dram_tensor("dsc", [704, D], FP16, kind="ExternalInput").ap()
    ident_in = nc.dram_tensor("ident", [128, 128], BF16, kind="ExternalInput").ap()
    identf_in = nc.dram_tensor("identf", [16, 16], F32, kind="ExternalInput").ap()
    oneh_in = nc.dram_tensor("oneh", [8, 1], BF16, kind="ExternalInput").ap()
    out_t = nc.dram_tensor("out", [TK, D], F32, kind="ExternalOutput").ap()

    dbg = "ExternalOutput" if debug else "Internal"
    sdt = U16 if debug else BF16
    _bfview = (lambda a: a.bitcast(BF16)) if debug else (lambda a: a)
    gu_scr = _bfview(nc.dram_tensor("gu_scr", [22, 128, 16, 512], sdt, kind=dbg).ap())
    dn_scr = _bfview(nc.dram_tensor("dn_scr", [16, 128, NKI, 128], sdt, kind=dbg).ap())
    wo_scr = nc.dram_tensor("wo_scr", [16, 128, D], BF16, kind="Internal").ap()
    wifg_scr = nc.dram_tensor("wifg_scr", [3, 16, 128, D], BF16, kind="Internal").ap()
    h_scr = _bfview(nc.dram_tensor("h_scr", [D, TK], sdt, kind=dbg).ap())
    a_scr = _bfview(nc.dram_tensor("a_scr", [D, TK], sdt, kind=dbg).ap())
    gl_scr = _bfview(nc.dram_tensor("gl_scr", [D, TK], sdt, kind=dbg).ap())
    hact_scr = _bfview(nc.dram_tensor("hact_scr", [I_FFN, TK], sdt, kind=dbg).ap())
    ffn_scr = _bfview(nc.dram_tensor("ffn_scr", [D, TK], sdt, kind=dbg).ap())
    rstdf_scr = nc.dram_tensor("rstdf_scr", [NTB, TB], F32, kind=dbg).ap()
    xn_dump = stat_dump = None
    if debug:
        xn_dump = nc.dram_tensor("xn_dump", [128, 65536], U16, kind="ExternalOutput").ap()
        stat_dump = nc.dram_tensor("stat_dump", [128, 192], F32, kind="ExternalOutput").ap()
    cc_in = nc.dram_tensor("cc_in", [128, 16], F32, kind="Internal").ap()
    cc_out = nc.dram_tensor("cc_out", [8 * 128, 16], F32, kind="Internal").ap()

    P = Prog()
    from contextlib import ExitStack
    with ExitStack() as es:
        ten = lambda *a: es.enter_context(nc.sbuf_tensor(*a))
        XN = ten("XN", [128, 131072], I8)
        U = ten("U", [128, 53248], I8)
        W = ten("W", [128, 21504], I8)
        ident = ten("s_ident", [128, 128], BF16)
        ones1 = ten("ones1", [1, 128], F32)
        onesb = ten("onesb", [128, 1], BF16)
        gnw = ten("s_gnw", [128, 16], F32)
        oneh = ten("s_oneh", [8, 1], BF16)
        sfin = ten("sfin", [128, 16], F32)
        carry = ten("carry", [128, 16], F32)
        smal = ten("smal", [128, 16], F32)
        srow = ten("srow", [1, 1024], F32)

        pb = [es.enter_context(nc.psum_tensor(f"pb{i}", [128, 512], F32))
              for i in range(8)]
        sems = {s: es.enter_context(nc.semaphore(s)) for s in Prog.SEMS}
        block = es.enter_context(nc.Block())

        bfv = lambda t, off, n: t[:, off:off + 2 * n].bitcast(BF16)
        f32v = lambda t, off, n: t[:, off:off + 4 * n].bitcast(F32)
        i32v = lambda t, off, n: t[:, off:off + 4 * n].bitcast(I32)
        u16v = lambda t, off, n: t[:, off:off + 2 * n].bitcast(U16)
        fp16v = lambda t, off, n: t[:, off:off + 2 * n].bitcast(FP16)

        XNbf = XN[:, :].bitcast(BF16)

        e_id = P.op("GD", lambda e: e.dma_start(ident[:], ident_in), sem="GD")
        e_oh = P.op("GD", lambda e: e.dma_start(oneh[:], oneh_in), sem="GD")
        gnw_r = ten("gnw_r", [16, 128], F32)
        identf = ten("s_identf", [16, 16], F32)
        e_gnl = P.op("GD", lambda e: e.dma_start(
            gnw_r[:], gnw_in.rearrange("(n p) -> n p", n=16)), sem="GD")
        e_idf = P.op("GD", lambda e: e.dma_start(identf[:], identf_in), sem="GD")
        e_gnt = P.op("PE", lambda e: e.transpose(pb[0][:, 0:16], gnw_r[:], identf[:]),
                     waits=[e_gnl, e_idf], sem="PE")
        e_gn = P.op("DV", lambda e: e.tensor_copy(gnw[:], pb[0][:, 0:16]),
                    waits=[e_gnt], sem="DV")
        e_c1 = P.op("DV", lambda e: e.memset(ones1[:], 1.0), sem="DV")
        e_c2 = P.op("DV", lambda e: e.memset(onesb[:], 1.0), sem="DV")
        e_cz = P.op("DV", lambda e: e.memset(carry[:], 0.0), sem="DV")

        def newton_rsqrt(q0, q1, tmp, scale_mul, eps, src, waits):
            P.op("DV", (lambda: lambda e: e.tensor_scalar(
                q0, src, scale_mul, eps, op0=ALU.mult, op1=ALU.add))(), waits=waits)
            P.op("DV", lambda e: e.tensor_scalar(q1.bitcast(I32), q0.bitcast(I32), 1, None,
                                                 op0=ALU.logical_shift_right))
            P.op("DV", lambda e: e.tensor_scalar(q1.bitcast(I32), q1.bitcast(I32), -1, MAGIC,
                                                 op0=ALU.mult, op1=ALU.add))
            ev = None
            for it in range(2):
                P.op("DV", lambda e: e.tensor_tensor(tmp, q1, q1, op=ALU.mult))
                P.op("DV", lambda e: e.tensor_tensor(tmp, tmp, q0, op=ALU.mult))
                P.op("DV", lambda e: e.tensor_scalar(tmp, tmp, -0.5, 1.5, op0=ALU.mult, op1=ALU.add))
                ev = P.op("DV", lambda e: e.tensor_tensor(q1, q1, tmp, op=ALU.mult),
                          sem="DV" if it == 1 else None)
            return ev

        CVF0, CVF1, CVB = 32768, 40960, 49152
        wo_prev = None
        wifg_done = {}
        conv_cv = [None, None]
        conv_st = [None]
        conv_srcs = [(w, n, s) for n in range(16) for w, s in ((0, wi_in), (1, wf_in), (2, wg_in))]
        conv_srcs += [(3, n, wo_in) for n in range(16)]
        conv_i = [0]

        conv_st = [None, None]

        def conv_step(k=1):
            for _ in range(k):
                if conv_i[0] >= len(conv_srcs):
                    return
                idx = conv_i[0]
                conv_i[0] += 1
                wj, n, wsrc = conv_srcs[idx]
                par = idx % 2
                f32t = f32v(U, (CVF0, CVF1)[par], 2048)
                bft = bfv(U, CVB, 2048) if par == 0 else bfv(W, 17408, 2048)
                le = P.op("GD", (lambda d_, s_, n_: lambda e: e.dma_start(
                    d_, s_[n_].rearrange("p k j -> p (k j)").bitcast(F32)))(f32t, wsrc, n),
                    waits=[conv_cv[par]], sem="GD")
                cv = P.op("GD", (lambda d_, s_: lambda e: e.tensor_copy(d_, s_))(bft, f32t),
                          waits=[le, conv_st[par]], sem="CC")
                conv_cv[par] = cv
                dst = wo_scr[n] if wj == 3 else wifg_scr[wj, n]
                se = P.op("SD", (lambda d_, s_: lambda e: e.dma_start(d_, s_))(dst, bft),
                          waits=[cv], sem="SD")
                conv_st[par] = se
                wifg_done[(n, wj) if wj != 3 else ("wo", n)] = se


        XT0, XT1, SQ0, XNT0, XNT1 = 0, 8192, 16384, 24576, 28672
        statx = ten("statx", [128, 192], F32)
        CS, CQ, CM, CV, CR, CT = 0, 32, 64, 96, 128, 160
        s0a_rd = [None] * 32
        s0a_sq = [None] * 32
        s0a_pool_l = [None] * 32
        for i in range(32):
            xtm = f32v(U, (XT0, XT1)[i % 2], 2048)
            sq = f32v(U, (SQ0, SQ0 + 8192)[i % 2], 1024)
            t0 = i * 128
            lx = P.op("SD", (lambda x_, t_: lambda e: e.dma_start(
                x_, x_in[t_:t_ + 128, :]))(xtm, t0),
                waits=[s0a_rd[i - 2], s0a_sq[i - 2], s0a_pool_l[i - 2]] if i >= 2 else (),
                sem="SD")
            sqb = f32v(U, (SQ0, SQ0 + 8192)[i % 2] + 4096, 1024)
            e_sq = P.op("AC", (lambda x_, s_, s2_: lambda e: (
                e.activation(s_, x_[:, 0:1024], AF.Square),
                e.activation(s2_, x_[:, 1024:2048], AF.Square))[-1])(xtm, sq, sqb),
                waits=[lx, s0a_rd[i - 2] if i >= 2 else None], sem="AC")
            s0a_sq[i] = e_sq
            s0a_pool = P.op("DV", (lambda x_, i_: lambda e: e.reduce_sum(
                statx[:, CS + i_:CS + i_ + 1], x_, axis=mybir.AxisListType.X))(xtm, i),
                waits=[lx], sem="DV")
            P.op("DV", (lambda s_, i_: lambda e: e.reduce_sum(
                statx[:, CQ + i_:CQ + i_ + 1], s_, axis=mybir.AxisListType.X))(sq, i),
                waits=[e_sq])
            s0a_rd[i] = P.op("DV", (lambda s_, i_: lambda e: e.reduce_sum(
                statx[:, CT + i_:CT + i_ + 1], s_, axis=mybir.AxisListType.X))(sqb, i),
                sem="DV")
            s0a_pool_l[i] = s0a_pool
            conv_step(1)

        def sp():
            P.op("DV", lambda e: e.tensor_copy(f32v(U, SQ0, 512), f32v(U, SQ0 + 4096, 512)))

        S_ = statx[:, CS:CS + 32]
        Q_ = statx[:, CQ:CQ + 32]
        M_ = statx[:, CM:CM + 32]
        V_ = statx[:, CV:CV + 32]
        R_ = statx[:, CR:CR + 32]
        T_ = statx[:, CT:CT + 32]
        sp()
        P.op("DV", lambda e: e.tensor_scalar(M_, S_, 1.0 / D, None, op0=ALU.mult),
             waits=[s0a_rd[31], s0a_sq[31], s0a_pool_l[31]])
        sp()
        P.op("DV", lambda e: e.tensor_tensor(Q_, Q_, T_, op=ALU.add))
        sp()
        P.op("DV", lambda e: e.tensor_scalar(V_, Q_, 1.0 / D, LN_EPS, op0=ALU.mult, op1=ALU.add))
        sp()
        P.op("DV", lambda e: e.tensor_tensor(T_, M_, M_, op=ALU.mult))
        sp()
        P.op("DV", lambda e: e.tensor_sub(V_, V_, T_))
        sp()
        P.op("DV", lambda e: e.tensor_scalar(R_.bitcast(I32), V_.bitcast(I32), 1, None,
                                             op0=ALU.logical_shift_right))
        sp()
        P.op("DV", lambda e: e.tensor_scalar(R_.bitcast(I32), R_.bitcast(I32), -1, MAGIC,
                                             op0=ALU.mult, op1=ALU.add))
        sp()
        for _ in range(2):
            P.op("DV", lambda e: e.tensor_tensor(T_, R_, R_, op=ALU.mult))
            sp()
            P.op("DV", lambda e: e.tensor_tensor(T_, T_, V_, op=ALU.mult))
            sp()
            P.op("DV", lambda e: e.tensor_scalar(T_, T_, -0.5, 1.5, op0=ALU.mult, op1=ALU.add))
            sp()
            P.op("DV", lambda e: e.tensor_tensor(R_, R_, T_, op=ALU.mult))
            sp()
        r_evt = P.op("DV", lambda e: e.tensor_copy(f32v(U, SQ0, 512), f32v(U, SQ0 + 4096, 512)),
                     sem="DV")

        s0_xn = [None] * 32
        s0_pe = [None] * 32
        s0_ev = [None] * 32
        for i in range(32):
            xtm = f32v(U, (XT0, XT1)[i % 2], 2048)
            xnt = bfv(U, (XNT0, XNT1)[i % 2], 2048)
            t0 = i * 128
            lx = P.op("SD", (lambda x_, t_: lambda e: e.dma_start(
                x_, x_in[t_:t_ + 128, :]))(xtm, t0),
                waits=[s0_xn[i - 2]] if i >= 2 else [s0a_rd[31], s0a_sq[31]], sem="SD")
            s0c_lx = lx
            s0_xn[i] = P.op("DV", (lambda x_, xn_, i_: lambda e: e.tensor_scalar(
                xn_, x_, statx[:, CM + i_:CM + i_ + 1], statx[:, CR + i_:CR + i_ + 1],
                op0=ALU.subtract, op1=ALU.mult))(xtm, xnt, i),
                waits=[lx, r_evt] + ([s0_pe[i - 2]] if i >= 2 else []), sem="DV")
            par = i % 2
            bA, bB = pb[4 + par * 2], pb[5 + par * 2]
            for k in range(16):
                c, j = k // 8, k % 8
                srcv = xnt[:, 1024 * c + j: 1024 * c + j + 1017:8]
                dst = (bA if k < 8 else bB)[:].bitcast(BF16)[:, (k % 8) * 128:(k % 8) * 128 + 128]
                ev = P.op("PE", (lambda d_, s_: lambda e: e.transpose(d_, s_, ident[:]))(dst, srcv),
                          waits=[s0_xn[i], e_id, s0_ev[i - 2] if i >= 2 else None],
                          sem="PE" if k == 15 else None)
            s0_pe[i] = ev
            for half in range(2):
                bank = (bA, bB)[half]
                srcv = bank[:].bitcast(BF16).rearrange("p (k t) -> p k t", k=8)
                dstv = XNbf.rearrange("p (k t) -> p k t", k=16)[:, half * 8:half * 8 + 8, t0:t0 + 128]
                ev = P.op("DV", (lambda d_, s_: lambda e: e.tensor_copy(d_, s_))(dstv, srcv),
                          waits=[s0_pe[i]], sem="DV" if half == 1 else None)
            s0_ev[i] = ev
            conv_step(1)
        conv_step(64)
        wo_scr_done = conv_st[0]
        xn_ready = s0_ev[31]
        if debug:
            P.op("SD", lambda e: e.dma_start(xn_dump, XNbf.bitcast(U16)),
                 waits=[xn_ready], sem="SD")
            P.op("SD", lambda e: e.dma_start(stat_dump, statx[:, 0:192]),
                 waits=[xn_ready], sem="SD")

        PK = (0, 2048)
        SC = (4096, 5120)
        LOW, FB, GB, SB, BITS = 6144, 8192, 10240, 12288, 14336
        VAL = (15360, 16384)
        wq_state = {"par": 0, "vpar": 0, "valdma": [None, None], "valevt": None,
                    "lastread": [None, None]}
        wq_done_gu = {}
        wq_done_dn = {}
        wq_queue = []

        def gen_dequant_planes(kind, c, cb):
            size = 64 if (kind == "dn" and c == 5) else 128
            par = wq_state["par"]
            wq_state["par"] ^= 1
            pk = i32v(W, PK[par], 512)[0:size, :]
            sc = fp16v(W, SC[par], 512)[0:size, :]
            if kind == "gu":
                src_pk = gup_in[128 * c:128 * c + size, 512 * cb:512 * cb + 512]
                src_sc = gsc_in[128 * c:128 * c + size, 512 * cb:512 * cb + 512]
            else:
                src_pk = dpk_in[128 * c:128 * c + size, 512 * cb:512 * cb + 512]
                src_sc = dsc_in[128 * c:128 * c + size, 512 * cb:512 * cb + 512]
            wpk = [wq_state["lastread"][par]]
            e_pk = P.op("GD", (lambda d_, s_: lambda e: e.dma_start(d_, s_))(pk, src_pk),
                        waits=wpk, sem="GD")
            e_sc = P.op("GD", (lambda d_, s_: lambda e: e.dma_start(d_, s_))(sc, src_sc),
                        waits=wpk, sem="GD")
            for j in range(8):
                sh = 4 * j
                low = i32v(W, LOW, 512)[0:size, :]
                fb = i32v(W, FB, 512)[0:size, :]
                gb = i32v(W, GB, 512)[0:size, :]
                sb = i32v(W, SB, 512)[0:size, :]
                bits = u16v(W, BITS, 512)[0:size, :]
                vpar = wq_state["vpar"]
                wq_state["vpar"] ^= 1
                val = bfv(W, VAL[vpar], 512)[0:size, :]
                P.op("DV", (lambda o_, i_, s_: lambda e: e.tensor_scalar(
                    o_, i_, s_, 7, op0=ALU.logical_shift_right, op1=ALU.bitwise_and))(low, pk, sh),
                    waits=[e_pk, e_sc, wq_state["valevt"]])
                P.op("DV", (lambda o_, i_: lambda e: e.tensor_scalar(
                    o_, i_, 64, 16128, op0=ALU.mult, op1=ALU.add))(fb, low))
                P.op("DV", (lambda o_, i_, f_: lambda e: e.scalar_tensor_tensor(
                    o_, i_, 16128, f_, op0=ALU.mult, op1=ALU.min))(gb, low, fb))
                if sh <= 12:
                    P.op("DV", (lambda o_, i_, s_: lambda e: e.tensor_scalar(
                        o_, i_, 12 - s_, 0x8000, op0=ALU.logical_shift_left,
                        op1=ALU.bitwise_and))(sb, pk, sh))
                else:
                    P.op("DV", (lambda o_, i_, s_: lambda e: e.tensor_scalar(
                        o_, i_, s_ - 12, 0x8000, op0=ALU.logical_shift_right,
                        op1=ALU.bitwise_and))(sb, pk, sh))
                P.op("DV", (lambda o_, a_, b_: lambda e: e.tensor_tensor(
                    o_, a_, b_, op=ALU.add))(bits, gb, sb))
                vev = P.op("DV", (lambda o_, b_, s_: lambda e: e.tensor_tensor(
                    o_, b_.bitcast(BF16), s_, op=ALU.mult))(val, bits, sc),
                    waits=[wq_state["valdma"][vpar]], sem="DV")
                wq_state["valevt"] = vev
                if j == 7:
                    wq_state["lastread"][par] = vev
                if kind == "gu":
                    dst = gu_scr[cb, 0:size, c * 8 + j, :]
                    sev = P.op("GD", (lambda d_, v_: lambda e: e.dma_start(d_, v_))(dst, val),
                               waits=[vev], sem="GD")
                    wq_state["valdma"][vpar] = sev
                    wq_done_gu.setdefault(cb, []).append(sev)
                else:
                    if c < 5:
                        kf, ro = 8 * c + j, 0
                    else:
                        kf, ro = 40 + j // 2, 64 * (j % 2)
                    for hh in range(4):
                        dst = dn_scr[4 * cb + hh, ro:ro + size, kf, :]
                        sev = P.op("GD", (lambda d_, v_, h_: lambda e: e.dma_start(
                            d_, v_[:, h_ * 128:h_ * 128 + 128]))(dst, val, hh),
                            waits=[vev], sem="GD")
                        wq_state["valdma"][vpar] = sev
                        wq_done_dn.setdefault(4 * cb + hh, []).append(sev)
                yield

        wq_gen = [None]

        def drain_wq(nplanes):
            while nplanes > 0:
                if wq_gen[0] is None:
                    if not wq_queue:
                        return
                    kind, c, cb = wq_queue.pop(0)
                    wq_gen[0] = gen_dequant_planes(kind, c, cb)
                try:
                    next(wq_gen[0])
                    nplanes -= 1
                except StopIteration:
                    wq_gen[0] = None

        wq_queue.extend([("gu", c, cb) for cb in range(22) for c in range(2)])

        WSL = [0, 4096, 8192, 12288, 16384, 20480]
        SI, SN, AFO, SG, IT1, IT2 = 24576, 26624, 28672, 30720, 32768, 34816
        H_ = (36864, 38912)
        A_ = (40960, 43008)
        GL = (45056, 46080)
        HB_ = (47104, 48128)
        AB_ = (49152, 50176)
        wload = {}
        slot_mm = {}
        psum_reader = {}
        h_dma = [None, None]
        a_dma = [None, None]
        g_dma = [None, None]
        sfin_evts = []
        prev = {"it1": None, "it2": None, "dh": None, "da": None, "gl": None}
        s2_entry = [s0_ev[31], s0a_sq[31], s0c_lx, wo_scr_done]

        for n in range(16):
            for wi_ in (0, 1, 2):
                slot = (n % 2) * 3 + wi_
                wbuf = bfv(U, WSL[slot], 2048)
                wload[(n, wi_)] = P.op("SD", (lambda d_, w_, n_: lambda e: e.dma_start(
                    d_, wifg_scr[w_, n_]))(wbuf, wi_, n),
                    waits=[slot_mm.get(slot), wifg_done[(n, wi_)]]
                          + (s2_entry if n < 2 else []), sem="SD")
            for tb in range(NTB):
                par = tb % 2
                bi, bfp, bg = pb[par], pb[2 + par], pb[4 + par]
                mm = []
                for wi_, bank in ((0, bi), (1, bfp), (2, bg)):
                    slot = (n % 2) * 3 + wi_
                    wbuf = bfv(U, WSL[slot], 2048)
                    for k in range(NK):
                        ev = P.op("PE", (lambda o_, l_, r_, st_, sp_: lambda e: e.matmul(
                            o_, l_, r_, start=st_, stop=sp_))(
                            bank[:], wbuf[:, k * 128:(k + 1) * 128],
                            XNbf[:, k * 4096 + tb * TB: k * 4096 + tb * TB + TB],
                            k == 0, k == NK - 1),
                            waits=[wload[(n, wi_)], xn_ready,
                                   psum_reader.get(id(bank))] if k == 0 else (),
                            sem="PE" if k == NK - 1 else None)
                    mm.append(ev)
                    slot_mm[slot] = ev
                e_i, e_f, e_g = mm
                si = f32v(U, SI, 512)
                sn = f32v(U, SN, 512)
                af = f32v(U, AFO, 512)
                sg = f32v(U, SG, 512)
                it1 = f32v(U, IT1, 512)
                it2 = f32v(U, IT2, 512)
                a1 = P.op("AC", (lambda o_, b_: lambda e: e.activation(
                    o_, b_[:], AF.Sigmoid))(si, bi), waits=[e_i, prev["it1"]], sem="AC")
                a2 = P.op("AC", (lambda o_, b_: lambda e: e.activation(
                    o_, b_[:], AF.Sigmoid, scale=-1.0))(sn, bfp), waits=[e_f, prev["it2"]], sem="AC")
                a3 = P.op("AC", (lambda o_, b_: lambda e: e.activation(
                    o_, b_[:], AF.Sigmoid))(af, bfp), waits=[e_f, prev["da"]], sem="AC")
                a4 = P.op("AC", (lambda o_, b_: lambda e: e.activation(
                    o_, b_[:], AF.Sigmoid))(sg, bg), waits=[e_g, prev["gl"]], sem="AC")
                d_it1 = P.op("DV", (lambda o_, s_, b_: lambda e: e.tensor_tensor(
                    o_, s_, b_[:], op=ALU.mult))(it1, si, bi), waits=[a1], sem="DV")
                psum_reader[id(bi)] = d_it1
                d_it2 = P.op("DV", (lambda o_, a_, s_: lambda e: e.tensor_tensor(
                    o_, a_, s_, op=ALU.mult))(it2, it1, sn), waits=[a2], sem="DV")
                prev["it1"], prev["it2"] = d_it1, d_it2
                hT = f32v(U, H_[par], 512)
                aT = f32v(U, A_[par], 512)
                hprev = f32v(U, H_[1 - par], 512)
                aprev = f32v(U, A_[1 - par], 512)
                hinit = 0.0 if tb == 0 else hprev[:, TB - 1:TB]
                ainit = 1.0 if tb == 0 else aprev[:, TB - 1:TB]
                dh = P.op("DV", (lambda o_, a_, b_, i_: lambda e: e.tensor_tensor_scan(
                    o_, a_, b_, i_, op0=ALU.mult, op1=ALU.add))(hT, af, it2, hinit),
                    waits=[a3, h_dma[par]], sem="DV")
                da = P.op("DV", (lambda o_, a_, i_: lambda e: e.tensor_tensor_scan(
                    o_, a_, a_, i_, op0=ALU.mult, op1=ALU.min))(aT, af, ainit),
                    waits=[a_dma[par]], sem="DV")
                prev["dh"], prev["da"] = dh, da
                gl = bfv(U, GL[par], 512)
                dgl = P.op("DV", (lambda o_, s_, b_: lambda e: e.tensor_tensor(
                    o_, s_, b_[:], op=ALU.mult))(gl, sg, bg), waits=[a4, g_dma[par]], sem="DV")
                prev["gl"] = dgl
                psum_reader[id(bg)] = dgl
                psum_reader[id(bfp)] = a3
                sfw = []
                if tb == NTB - 1:
                    sf = P.op("DV", (lambda h_, n_: lambda e: e.tensor_copy(
                        sfin[:, n_:n_ + 1], h_[:, TB - 1:TB]))(hT, n), sem="DV")
                    sfin_evts.append(sf)
                    sfw = [sf]
                hbf = bfv(U, HB_[par], 512)
                abf = bfv(U, AB_[par], 512)
                cvh = P.op("DV", (lambda d_, s_: lambda e: e.tensor_copy(d_, s_))(hbf, hT),
                           waits=[h_dma[par]], sem="DV")
                cva = P.op("DV", (lambda d_, s_: lambda e: e.tensor_copy(d_, s_))(abf, aT),
                           waits=[a_dma[par]], sem="DV")
                h_dma[par] = P.op("GD", (lambda h_, n_, t_: lambda e: e.dma_start(
                    h_scr[128 * n_:128 * n_ + 128, TB * t_:TB * t_ + TB], h_))(hbf, n, tb),
                    waits=[cvh], sem="GD")
                a_dma[par] = P.op("GD", (lambda a_, n_, t_: lambda e: e.dma_start(
                    a_scr[128 * n_:128 * n_ + 128, TB * t_:TB * t_ + TB], a_))(abf, n, tb),
                    waits=[cva], sem="GD")
                g_dma[par] = P.op("GD", (lambda g_, n_, t_: lambda e: e.dma_start(
                    gl_scr[128 * n_:128 * n_ + 128, TB * t_:TB * t_ + TB], g_))(gl, n, tb),
                    waits=[dgl], sem="GD")
                drain_wq(3)
        s2_done_dmas = [h_dma[0], h_dma[1], a_dma[0], a_dma[1], g_dma[0], g_dma[1]]

        e_ccin = P.op("GD", lambda e: e.dma_start(cc_in, sfin[:]), waits=sfin_evts, sem="GD")
        e_cc = P.op("GD", lambda e: e.collective_compute(
            "AllGather", ALU.bypass, replica_groups=[[0, 1, 2, 3, 4, 5, 6, 7]],
            ins=[cc_in.opt()], outs=[cc_out.opt()]), waits=[e_ccin], sem="CC")
        sagb = W[0:8, 17408:21504].bitcast(BF16)
        e_sag = P.op("GD", lambda e: e.dma_start(
            sagb, cc_out.rearrange("(r p) n -> r (p n)", r=8)),
            waits=[e_cc, wo_scr_done], sem="GD")

        drain_wq(10 ** 6)
        wq_queue.extend([("dn", c, cb) for cb in range(4) for c in range(6)])

        GUS = [0, 16384, 32768]
        SIL, T1 = 49152, 50176
        HA = (51200, 52224)
        s3a_entry = P.barrier()
        gu_load = {}
        gu_slot_mm = {}
        hact_dma = {}
        first_slot_use = set()
        prev_sil = None
        prev_t1 = None
        s4_done = False
        carry_evt = None
        last_ha_evt = None

        def gu_block_load(bi_, slot):
            w = [gu_slot_mm.get(slot)] + wq_done_gu.get(bi_, [])
            if slot not in first_slot_use:
                first_slot_use.add(slot)
                w += s3a_entry
            dst = bfv(U, GUS[slot], 8192)
            return P.op("SD", (lambda d_, b_: lambda e: e.dma_start(
                d_, gu_scr[b_].rearrange("p k j -> p (k j)")))(dst, bi_),
                waits=w, sem="SD")

        gu_load[0] = gu_block_load(0, 0)
        gu_load[11] = gu_block_load(11, 1)
        slot_of = {0: 0, 11: 1}
        next_slot = 2
        for gb in range(11):
            if gb + 1 < 11:
                gu_load[gb + 1] = gu_block_load(gb + 1, next_slot)
                slot_of[gb + 1] = next_slot
            gslot, uslot = slot_of[gb], slot_of[gb + 11]
            gt = bfv(U, GUS[gslot], 8192)
            up = bfv(U, GUS[uslot], 8192)
            for oc in range(4):
                ci = 4 * gb + oc
                for tb in range(NTB):
                    par = tb % 2
                    bgt, bup = pb[par], pb[2 + par]
                    for k in range(NK):
                        eg = P.op("PE", (lambda o_, l_, r_, st_, sp_: lambda e: e.matmul(
                            o_, l_, r_, start=st_, stop=sp_))(
                            bgt[:], gt[:, k * 512 + oc * 128: k * 512 + oc * 128 + 128],
                            XNbf[:, k * 4096 + tb * TB: k * 4096 + tb * TB + TB],
                            k == 0, k == NK - 1),
                            waits=[gu_load[gb], psum_reader.get(id(bgt))] if k == 0 else (),
                            sem="PE" if k == NK - 1 else None)
                    for k in range(NK):
                        eu = P.op("PE", (lambda o_, l_, r_, st_, sp_: lambda e: e.matmul(
                            o_, l_, r_, start=st_, stop=sp_))(
                            bup[:], up[:, k * 512 + oc * 128: k * 512 + oc * 128 + 128],
                            XNbf[:, k * 4096 + tb * TB: k * 4096 + tb * TB + TB],
                            k == 0, k == NK - 1),
                            waits=[gu_load[gb + 11], psum_reader.get(id(bup))] if k == 0 else (),
                            sem="PE" if k == NK - 1 else None)
                    gu_slot_mm[gslot] = eg
                    gu_slot_mm[uslot] = eu
                    sil = bfv(U, SIL, 512)
                    asl = P.op("AC", (lambda o_, b_: lambda e: e.activation(
                        o_, b_[:], AF.Sigmoid))(sil, bgt),
                        waits=[eg, prev_t1, wo_scr_done], sem="AC")
                    t1 = bfv(U, T1, 512)
                    dt1 = P.op("DV", (lambda o_, s_, b_: lambda e: e.tensor_tensor(
                        o_, s_, b_[:], op=ALU.mult))(t1, sil, bgt), waits=[asl], sem="DV")
                    prev_t1 = dt1
                    psum_reader[id(bgt)] = dt1
                    ha = bfv(U, HA[par], 512)
                    dha = P.op("DV", (lambda o_, t_, b_: lambda e: e.tensor_tensor(
                        o_, t_, b_[:], op=ALU.mult))(ha, t1, bup),
                        waits=[eu, hact_dma.get(par)], sem="DV")
                    psum_reader[id(bup)] = dha
                    hact_dma[par] = P.op("GD", (lambda h_, c_, t_: lambda e: e.dma_start(
                        hact_scr[128 * c_:128 * c_ + 128, TB * t_:TB * t_ + TB], h_))(ha, ci, tb),
                        waits=[dha], sem="GD")
                    drain_wq(1)
                drain_wq(2)
            if gb + 12 < 22:
                free = slot_of[gb]
                gu_load[gb + 12] = gu_block_load(gb + 12, free)
                slot_of[gb + 12] = free
                next_slot = slot_of[gb + 11]
                del slot_of[gb]
            if gb == 2 and not s4_done:
                s4_done = True
                for nn in range(16):
                    evp = P.op("PE", (lambda nn_: lambda e: e.matmul(
                        pb[7][:, nn_:nn_ + 1], sagb[:, nn_::16], oneh[:],
                        start=True, stop=True))(nn),
                        waits=[e_sag, e_oh], sem="PE" if nn == 15 else None)
                carry_evt = P.op("DV", lambda e: e.tensor_copy(carry[:], pb[7][:, 0:16]),
                                 waits=[evp, e_cz], sem="DV")
                psum_reader[id(pb[7])] = carry_evt
        drain_wq(10 ** 6)
        s3a_done = [hact_dma[0], hact_dma[1]]
        gu_last_pe = [gu_slot_mm[s] for s in gu_slot_mm]

        s3b_entry = P.barrier()
        DN = (0, 11264)
        HAC = (22528, 67584)
        dn_slot_mm = [None, None]
        y2_rd = [None, None]
        hac_last = [None, None]
        ffn_dma = {}
        stats_rd = None
        rstdf_gd = None
        q0, q1 = srow[:, 0:512], srow[:, 512:1024]
        tmp_row = f32v(U, 43008, 512)[0:1, :]
        hl = [None, None]

        def hac_load(t_):
            par = t_ % 2
            return P.op("SD", (lambda d_, tt_: lambda e: e.dma_start(
                d_, hact_scr.rearrange("(k p) t -> p k t", p=128)[:, :, TB * tt_:TB * tt_ + TB]))(
                bfv(XN, HAC[par], 22528), t_),
                waits=(s3b_entry if t_ == 0 else []) + [hac_last[par]] + s3a_done, sem="SD")

        hl[0] = hac_load(0)
        for tb in range(NTB):
            hac = bfv(XN, HAC[tb % 2], 22528)
            stats = pb[6]
            est = None
            for m in range(16):
                slot = m % 2
                dnb = bfv(XN, DN[slot], 5632)
                dl = P.op("SD", (lambda d_, o_: lambda e: e.dma_start(
                    d_, dn_scr[o_].rearrange("p k j -> p (k j)")))(dnb, m),
                    waits=(s3b_entry if tb == 0 else []) + [dn_slot_mm[slot]]
                          + (wq_done_dn.get(m, []) if tb == 0 else []), sem="SD")
                par = m % 2
                by = pb[par]
                for k in range(NKI):
                    ey = P.op("PE", (lambda o_, l_, r_, st_, sp_: lambda e: e.matmul(
                        o_, l_, r_, start=st_, stop=sp_))(
                        by[:], dnb[:, k * 128:(k + 1) * 128],
                        hac[:, k * TB: k * TB + TB], k == 0, k == NKI - 1),
                        waits=[dl, hl[tb % 2], psum_reader.get(id(by))] if k == 0 else (),
                        sem="PE" if k == NKI - 1 else None)
                dn_slot_mm[slot] = ey
                hac_last[tb % 2] = ey
                yb = bfv(U, (49152, 50176)[par], 512)
                y2 = bfv(U, (51200, 52224)[par], 512)
                ac1 = P.op("AC", (lambda o_, b_: lambda e: e.activation(
                    o_, b_[:], AF.Copy))(yb, by), waits=[ey, ffn_dma.get(par)], sem="AC")
                ac2 = P.op("AC", (lambda o_, b_: lambda e: e.activation(
                    o_, b_[:], AF.Square))(y2, by), waits=[ey, y2_rd[par]], sem="AC")
                psum_reader[id(by)] = ac2
                est = P.op("PE", (lambda y_, st_, sp_: lambda e: e.matmul(
                    stats[0:1, 0:TB], onesb[:], y_, start=st_, stop=sp_))(y2, m == 0, m == 15),
                    waits=[ac2, e_c2] + ([stats_rd] if m == 0 else []),
                    sem="PE")
                y2_rd[par] = est
                ffn_dma[par] = P.op("GD", (lambda y_, m_, t_: lambda e: e.dma_start(
                    ffn_scr[128 * m_:128 * m_ + 128, TB * t_:TB * t_ + TB], y_))(yb, m, tb),
                    waits=[ac1], sem="GD")
                if m == 1 and tb + 1 < NTB:
                    hl[(tb + 1) % 2] = hac_load(tb + 1)
            stats_rd = newton_rsqrt(q0, q1, tmp_row, 1.0 / D, FFN_EPS, stats[0:1, 0:TB],
                                    waits=[est, rstdf_gd] + (s3b_entry if tb == 0 else []))
            rstdf_gd = P.op("GD", (lambda t_: lambda e: e.dma_start(
                rstdf_scr[t_:t_ + 1, :], q1))(tb), waits=[stats_rd], sem="GD")
        s3b_ffn_done = [ffn_dma[0], ffn_dma[1]]

        s5_entry = P.barrier()
        HS, AS, GS, FS, HG, OM0, XT = 0, 16384, 32768, 49152, 65536, 81920, 98304
        WOS = (0, 4096)
        OTM = (8192, 16384, 24576, 32768)
        SUM = (40960, 41984)
        SQT = (43008, 44032)
        TMP2, RBH = 45056, 47104
        RBF = (49152, 51200)
        h_sb = bfv(XN, HS, 8192)
        a_sb = bfv(XN, AS, 8192)
        g_sb = bfv(XN, GS, 8192)
        f_sb = bfv(XN, FS, 8192)
        hg_sb = bfv(XN, HG, 8192)
        om_of = lambda t_: bfv(XN, OM0, 8192) if t_ % 2 == 0 else bfv(W, 0, 8192)
        rbh = f32v(U, RBH, 512)
        tmp2 = f32v(U, TMP2, 512)
        rbh_row = rbh[0:1, :]

        st5 = {
            "ld_h": None, "ld_g": None, "ld_f": None, "d1_last": {},
            "aq_last": None, "stats_rd": stats_rd, "sq_rd": [None, None],
            "sm_rd": [None, None], "q_guard": None, "wo_prev": [None, None],
            "out_prev": {}, "om_done": {}, "hg_last": {}, "x_evts": {},
        }
        last_out = [None]

        def s5_loads(tb):
            first = (s5_entry if tb == 0 else []) + (s2_done_dmas if tb == 0 else [])

            def scr_load(dst, scr, extra):
                return P.op("SD", (lambda d_, s_, t_: lambda e: e.dma_start(
                    d_, s_.rearrange("(k p) t -> p k t", p=128)[:, :, TB * t_:TB * t_ + TB]))(dst, scr, tb),
                    waits=extra + first, sem="SD")

            lh = scr_load(h_sb, h_scr, [st5["ld_h"]])
            la = scr_load(a_sb, a_scr, [st5["ld_h"]])
            lg = scr_load(g_sb, gl_scr, [st5["ld_g"]])
            rbf = f32v(U, RBF[tb % 2], 512)
            lr = P.op("GD", (lambda t_, r_: lambda e: e.dma_start(
                r_[0:1, :], rstdf_scr[t_:t_ + 1, :]))(tb, rbf),
                waits=[rstdf_gd, st5["d1_last"].get(tb - 2)] + (s5_entry if tb == 0 else []),
                sem="GD")
            return lh, la, lg, lr

        def s5_hg(tb, lh, la):
            hg_evts = []
            aq = None
            for n in range(16):
                dhg = P.op("DV", (lambda n_: lambda e: e.scalar_tensor_tensor(
                    hg_sb[:, n_ * TB:n_ * TB + TB], a_sb[:, n_ * TB:n_ * TB + TB],
                    carry[:, n_:n_ + 1], h_sb[:, n_ * TB:n_ * TB + TB],
                    op0=ALU.mult, op1=ALU.add))(n),
                    waits=[lh, la, carry_evt, st5["aq_last"]], sem="DV")
                sqt = bfv(U, SQT[n % 2], 512)
                aq = P.op("AC", (lambda n_, s_: lambda e: e.activation(
                    s_, hg_sb[:, n_ * TB:n_ * TB + TB], AF.Square))(n, sqt),
                    waits=[dhg, st5["sq_rd"][n % 2]] + (s5_entry if tb == 0 and n < 2 else []),
                    sem="AC")
                hg_evts.append((dhg, aq))
            st5["aq_last"] = aq
            st5["hg_last"][tb] = hg_evts[15][0]
            return hg_evts

        def s5_stats(tb, hg_evts):
            stats = pb[6]
            est = None
            for n in range(16):
                sqt = bfv(U, SQT[n % 2], 512)
                est = P.op("PE", (lambda s_, st_, sp_: lambda e: e.matmul(
                    stats[0:1, 0:TB], onesb[:], s_, start=st_, stop=sp_))(sqt, n == 0, n == 15),
                    waits=[hg_evts[n][1]] + ([st5["stats_rd"]] if n == 0 else []),
                    sem="PE")
            st5["sq_rd"][0] = est
            st5["sq_rd"][1] = est
            return est

        def s5_rstd(tb, est, lr):
            st5["stats_rd"] = newton_rsqrt(q0, q1, rbh_row, 1.0 / D, RMS_EPS,
                                           pb[6][0:1, 0:TB],
                                           waits=[est, st5["q_guard"],
                                                  Evt("GD", lr.val, "none", True)])
            rbf = f32v(U, RBF[tb % 2], 512)
            eb1 = P.op("PE", lambda e: e.matmul(pb[7][:, 0:TB], ones1[:], q1, start=True, stop=True),
                       waits=[st5["stats_rd"], e_c1, psum_reader.get(id(pb[7]))], sem="PE")
            eb2 = P.op("PE", (lambda r_: lambda e: e.matmul(
                pb[3][:, 0:TB], ones1[:], r_[0:1, 0:TB], start=True, stop=True))(rbf),
                waits=[lr, e_c1, psum_reader.get(id(pb[3]))], sem="PE")
            ebh = P.op("DV", lambda e: e.tensor_copy(rbh, pb[7][:]), waits=[eb1], sem="DV")
            ebf = P.op("DV", (lambda r_: lambda e: e.tensor_copy(r_, pb[3][:]))(rbf),
                       waits=[eb2], sem="DV")
            psum_reader[id(pb[7])] = ebh
            psum_reader[id(pb[3])] = ebf
            st5["q_guard"] = ebh
            return ebh, ebf

        def s5_omat(tb, lg, ebh):
            om_sb = om_of(tb)
            om_done = None
            for n in range(16):
                P.op("DV", (lambda n_: lambda e: e.tensor_tensor(
                    hg_sb[:, n_ * TB:n_ * TB + TB], hg_sb[:, n_ * TB:n_ * TB + TB],
                    g_sb[:, n_ * TB:n_ * TB + TB], op=ALU.mult))(n),
                    waits=[lg, Evt("AC", st5["aq_last"].val, "none", False)])
                om_done = P.op("DV", (lambda n_, o_: lambda e: e.scalar_tensor_tensor(
                    o_[:, n_ * TB:n_ * TB + TB], hg_sb[:, n_ * TB:n_ * TB + TB],
                    gnw[:, n_:n_ + 1], rbh, op0=ALU.mult, op1=ALU.mult))(n, om_sb),
                    waits=[ebh, e_gn] + (s5_entry if tb <= 1 and n == 0 else []),
                    sem="DV" if n == 15 else None)
            st5["om_done"][tb] = om_done
            st5["ld_g"] = om_done

        def s5_prep(tb, upto):
            if upto == 0:
                st5["_l"] = s5_loads(tb)
                st5["_hg"] = s5_hg(tb, st5["_l"][0], st5["_l"][1])
                st5["ld_h"] = st5["hg_last"][tb]
            elif upto == 1:
                st5["_est"] = s5_stats(tb, st5["_hg"])
            else:
                lh, la, lg, lr = st5["_l"]
                ebh, ebf = s5_rstd(tb, st5["_est"], lr)
                st5["_ebf"] = ebf
                s5_omat(tb, lg, ebh)

        def s5_main(tb):
            om_sb = om_of(tb)
            rbf = f32v(U, RBF[tb % 2], 512)
            ebf = st5["_main_ebf"]
            om_done = st5["om_done"][tb]
            lf = st5["_lf"]
            x_evts = []
            for qq in range(4):
                xv = f32v(XN, XT + qq * 8192, 2048)
                x_evts.append(P.op("SD", (lambda x_, t_, q_: lambda e: e.dma_start(
                    x_, x_in[t_ * TB + q_ * 128: t_ * TB + q_ * 128 + 128, :]))(xv, tb, qq),
                    waits=[st5["out_prev"].get(qq)] + (s5_entry if tb == 0 else []), sem="SD"))
            evac_last = None
            d1 = None
            d2 = None
            for m in range(16):
                par = m % 2
                wob = bfv(U, WOS[par], 2048)
                wl2 = P.op("SD", (lambda d_, m_: lambda e: e.dma_start(
                    d_, wo_scr[m_]))(wob, m),
                    waits=[st5["wo_prev"][par], wo_scr_done] +
                          (gu_last_pe + s5_entry if tb == 0 and m < 2 else []),
                    sem="SD")
                ba = pb[par]
                for k in range(NK):
                    ea = P.op("PE", (lambda o_, l_, r_, st_, sp_: lambda e: e.matmul(
                        o_, l_, r_, start=st_, stop=sp_))(
                        ba[:], wob[:, k * 128:(k + 1) * 128],
                        om_sb[:, k * TB:k * TB + TB], k == 0, k == NK - 1),
                        waits=[wl2, om_done, psum_reader.get(id(ba))] if k == 0 else (),
                        sem="PE" if k == NK - 1 else None)
                st5["wo_prev"][par] = ea
                sm = bfv(U, SUM[par], 512)
                d1 = P.op("DV", (lambda m_: lambda e: e.tensor_tensor(
                    tmp2, f_sb[:, m_ * TB:m_ * TB + TB], rbf, op=ALU.mult))(m),
                    waits=[lf, ebf], sem="DV" if m == 15 else None)
                d2 = P.op("DV", (lambda s_, b_: lambda e: e.tensor_tensor(
                    s_, tmp2, b_[:], op=ALU.add))(sm, ba),
                    waits=[ea, st5["sm_rd"][par]], sem="DV")
                psum_reader[id(ba)] = d2
                bt = pb[4 + par]
                for qq in range(4):
                    et = P.op("PE", (lambda d_, s_, q_: lambda e: e.transpose(
                        d_, s_[:, q_ * 128:q_ * 128 + 128], ident[:]))(
                        bt[:].bitcast(BF16)[:, qq * 128:qq * 128 + 128], sm, qq),
                        waits=[d2] + ([psum_reader.get(id(bt))] if qq == 0 else []),
                        sem="PE" if qq == 3 else None)
                st5["sm_rd"][par] = et
                for qq in range(4):
                    otm = f32v(U, OTM[qq], 2048)
                    xv = f32v(XN, XT + qq * 8192, 2048)
                    ev = P.op("DV", (lambda o_, q_, x_, m_, b_: lambda e: e.tensor_tensor(
                        o_[:, m_ * 128:m_ * 128 + 128],
                        b_[:].bitcast(BF16)[:, q_ * 128:q_ * 128 + 128],
                        x_[:, m_ * 128:m_ * 128 + 128], op=ALU.add))(otm, qq, xv, m, bt),
                        waits=[et, x_evts[qq], st5["out_prev"].get(qq)],
                        sem="DV" if qq == 3 else None)
                evac_last = ev
                psum_reader[id(bt)] = ev
                if tb + 1 < NTB:
                    if m == 1:
                        s5_prep(tb + 1, 0)
                    elif m == 6:
                        s5_prep(tb + 1, 1)
                    elif m == 9:
                        s5_prep(tb + 1, 2)
            st5["d1_last"][tb] = d1
            for qq in range(4):
                otm = f32v(U, OTM[qq], 2048)
                oe = P.op("GD", (lambda o_, t_, q_: lambda e: e.dma_start(
                    out_t[t_ * TB + q_ * 128: t_ * TB + q_ * 128 + 128, :], o_))(otm, tb, qq),
                    waits=[evac_last], sem="GD")
                st5["out_prev"][qq] = oe
                last_out[0] = oe
            if tb + 1 < NTB:
                lf2 = P.op("SD", (lambda t_: lambda e: e.dma_start(
                    f_sb.rearrange("p (k t) -> p k t", k=16),
                    ffn_scr.rearrange("(k p) t -> p k t", p=128)[:, :, TB * t_:TB * t_ + TB]))(tb + 1),
                    waits=[d1], sem="SD")
                st5["_lf"] = lf2
                st5["_main_ebf"] = st5["_ebf"]

        st5["_l"] = None
        s5_prep(0, 0)
        s5_prep(0, 1)
        s5_prep(0, 2)
        lf0 = P.op("SD", lambda e: e.dma_start(
            f_sb.rearrange("p (k t) -> p k t", k=16),
            ffn_scr.rearrange("(k p) t -> p k t", p=128)[:, :, 0:TB]),
            waits=s5_entry + s3b_ffn_done, sem="SD")
        st5["_lf"] = lf0
        st5["_main_ebf"] = st5["_ebf"]
        for tb in range(NTB):
            s5_main(tb)
        last_out = last_out[0]

        P.op("SD", lambda e: e.nop(),
             waits=[Evt("SD", P.counts["SD"], "none", True),
                    Evt("GD", P.counts["GD"], "none", True)])

        @block.sync
        def _(s):
            P.emit_engine("SD", s, sems)

        @block.gpsimd
        def _(g):
            P.emit_engine("GD", g, sems)

        @block.vector
        def _(v):
            P.emit_engine("DV", v, sems)

        @block.scalar
        def _(a):
            P.emit_engine("AC", a, sems)

        @block.tensor
        def _(t):
            P.emit_engine("PE", t, sems)

    return nc


def _host_prep(Wi, Wf, Wg, Wo, gup, gsc, dpk, dsc):
    pH = _perm_H()
    pI = _perm_I()
    wi_t = _wtile(Wi[pH])
    wf_t = _wtile(Wf[pH])
    wg_t = _wtile(Wg[pH])
    wo_t = _wtile(Wo)
    rows = np.concatenate([gup[:I_FFN][pI], gup[I_FFN:][pI]], axis=0)
    gup_T = np.ascontiguousarray(rows.T)
    scl = np.concatenate([gsc[:, :I_FFN][:, pI], gsc[:, I_FFN:][:, pI]], axis=1)
    gsc_exp = np.ascontiguousarray(np.repeat(scl, 16, axis=0))
    dpk_T = np.ascontiguousarray(dpk.T)
    dsc_exp = np.ascontiguousarray(np.repeat(dsc, 16, axis=0))
    return wi_t, wf_t, wg_t, wo_t, gup_T, gsc_exp, dpk_T, dsc_exp


def kernel(L0_attn_hidden_states, L0_attn_Wi, L0_attn_Wf, L0_attn_Wg, L0_attn_Wo,
           L0_attn_g_norm_weight, L0_ffn_gate_up_packed, L0_ffn_gate_up_scales,
           L0_ffn_down_packed, L0_ffn_down_scales):
    x = np.asarray(L0_attn_hidden_states, dtype=np.float32)
    wi_t, wf_t, wg_t, wo_t, gup_T, gsc_exp, dpk_T, dsc_exp = _host_prep(
        np.asarray(L0_attn_Wi, np.float32), np.asarray(L0_attn_Wf, np.float32),
        np.asarray(L0_attn_Wg, np.float32), np.asarray(L0_attn_Wo, np.float32),
        np.asarray(L0_ffn_gate_up_packed, np.int32), np.asarray(L0_ffn_gate_up_scales),
        np.asarray(L0_ffn_down_packed, np.int32), np.asarray(L0_ffn_down_scales))
    gnw = np.asarray(L0_attn_g_norm_weight, np.float32)
    ident = np.eye(128, dtype=ml_dtypes.bfloat16)

    nc = build()
    in_maps = []
    for c in range(8):
        b, half = c // 2, c % 2
        oh = np.zeros((8, 1), dtype=ml_dtypes.bfloat16)
        if half == 1:
            oh[c - 1, 0] = 1.0
        in_maps.append(dict(
            x=np.ascontiguousarray(x[b, half * TK:(half + 1) * TK, :]),
            wi=wi_t, wf=wf_t, wg=wg_t, wo=wo_t, gnw=gnw,
            gup=gup_T, gsc=gsc_exp, dpk=dpk_T, dsc=dsc_exp,
            ident=ident, identf=np.eye(16, dtype=np.float32), oneh=oh,
        ))
    res = run_bass_kernel_spmd(nc, in_maps, core_ids=list(range(8)))
    out = np.empty((4, 8192, D), np.float32)
    for c in range(8):
        b, half = c // 2, c % 2
        out[b, half * TK:(half + 1) * TK, :] = np.asarray(res.results[c]["out"])
    return out
